# version 18
# baseline (speedup 1.0000x reference)
import numpy as np
import ml_dtypes
from contextlib import ExitStack

import concourse.bass as bass
import concourse.mybir as mybir
from concourse.bass_utils import run_bass_kernel_spmd

F32 = mybir.dt.float32
BF16 = mybir.dt.bfloat16
I8 = mybir.dt.int8
I32 = mybir.dt.int32
AF = mybir.ActivationFunctionType
ALU = mybir.AluOpType
AX = mybir.AxisListType
BF = ml_dtypes.bfloat16

NCORES = 8
T, D, INNER = 16384, 1024, 4096
TPC = T // NCORES
NT = TPC // 128
KB = 1024
NB = T // KB
DC = 8
ICN = 32
NIN = 8
NCH = 4
CTL = NT // NCH
CHT = 128 * CTL
SCL = 1.0 / 32.0
EPS = 1e-5
NEG = -1e9
NWT = 32
N_IT = NB * (NB + 1) // 2


def build():
    nc = bass.Bass("TRN2", target_bir_lowering=False)

    x_own_d = nc.dram_tensor("x_own", [TPC, D], F32, kind="ExternalInput")
    x_ctg_d = nc.dram_tensor("x_ctg", [TPC, D], F32, kind="ExternalInput")
    wq_d = nc.dram_tensor("wq", [D, D], F32, kind="ExternalInput")
    wk_d = nc.dram_tensor("wk", [D, D], F32, kind="ExternalInput")
    wv_d = nc.dram_tensor("wv", [D, D], F32, kind="ExternalInput")
    bq_d = nc.dram_tensor("bq", [D], F32, kind="ExternalInput")
    bk_d = nc.dram_tensor("bk", [D], F32, kind="ExternalInput")
    bv_d = nc.dram_tensor("bv", [D], F32, kind="ExternalInput")
    wg_d = nc.dram_tensor("wg", [INNER, D], F32, kind="ExternalInput")
    wu_d = nc.dram_tensor("wu", [INNER, D], F32, kind="ExternalInput")
    wd_d = nc.dram_tensor("wd", [D, INNER], F32, kind="ExternalInput")
    cid_d = nc.dram_tensor("cid", [1, 1], F32, kind="ExternalInput")
    out_d = nc.dram_tensor("out", [TPC, D], F32, kind="ExternalOutput")

    xn_own_d = nc.dram_tensor("xn_own_s", [TPC, D], BF16)
    xn_ctg_d = nc.dram_tensor("xn_ctg_s", [TPC, D], BF16)
    KVSZ = D * TPC
    ccKVi = nc.dram_tensor("ccKVi", [2 * KVSZ], BF16)
    ccKVo = nc.dram_tensor("ccKVo", [NCORES, 2 * KVSZ], BF16, addr_space="Shared")
    sti = nc.dram_tensor("sti", [1, 16], F32)
    sto_d = nc.dram_tensor("sto", [NCORES, 16], F32, addr_space="Shared")
    wgt_d = nc.dram_tensor("wgt_s", [INNER, D], BF16)
    wut_d = nc.dram_tensor("wut_s", [INNER, D], BF16)
    wdt_d = nc.dram_tensor("wdt_s", [D, INNER], BF16)
    wgT_d = nc.dram_tensor("wgT_s", [D, INNER], BF16)
    wuT_d = nc.dram_tensor("wuT_s", [D, INNER], BF16)
    wgT8_d = nc.dram_tensor("wgT8_s", [D, INNER], I8)
    wuT8_d = nc.dram_tensor("wuT8_s", [D, INNER], I8)
    wdT_d = nc.dram_tensor("wdT_s", [INNER, D], BF16)
    tint_d = nc.dram_tensor("tint_s", [TPC, D], BF16)
    hint_d = nc.dram_tensor("hint_s", [TPC, INNER], BF16)
    xres_d = nc.dram_tensor("xres_s", [TPC, D], F32)
    col_d = nc.dram_tensor("col_s", [128], F32)
    sc2_d = nc.dram_tensor("sc2_s", [16], F32)
    wcol3_d = nc.dram_tensor("wcol3_s", [384], F32)
    wsc12_d = nc.dram_tensor("wsc12_s", [12], F32)

    ident_dr = nc.inline_tensor(np.eye(128, dtype=BF), "ident_i")

    with ExitStack() as ctx:
        e = ctx.enter_context

        ident = e(nc.sbuf_tensor("ident", [128, 128], BF16))
        mask = e(nc.sbuf_tensor("mask_sb", [128, KB], F32))
        bq_sb = e(nc.sbuf_tensor("bq_sb", [128, 8], F32))
        bk_sb = e(nc.sbuf_tensor("bk_sb", [128, 8], F32))
        eps_sb = e(nc.sbuf_tensor("eps_sb", [128, 1], F32))
        cid_b = e(nc.sbuf_tensor("cid_b", [128, 1], F32))
        thr = e(nc.sbuf_tensor("thr", [128, 1], F32))
        m_sb = e(nc.sbuf_tensor("m_sb", [128, NT], F32))
        z_sb = e(nc.sbuf_tensor("z_sb", [128, NT], F32))
        f_sb = e(nc.sbuf_tensor("f_sb", [128, NT], F32))
        a_sb = e(nc.sbuf_tensor("a_sb", [128, NT], F32))
        cd_sb = e(nc.sbuf_tensor("cd_sb", [128, NT], F32))
        scg_sb = e(nc.sbuf_tensor("scg_sb", [128, NT], F32))
        scu_sb = e(nc.sbuf_tensor("scu_sb", [128, NT], F32))
        ef16 = e(nc.sbuf_tensor("ef16", [128, NT], F32))
        rowmax = e(nc.sbuf_tensor("rowmax", [128, 1], F32))
        d2 = e(nc.sbuf_tensor("d2", [128, 2], F32))
        cfac_a = e(nc.sbuf_tensor("cfac_a", [128, 2], F32))
        cfac = e(nc.sbuf_tensor("cfac", [128, 2], F32))
        nbias = e(nc.sbuf_tensor("nbias", [128, 2], F32))
        zblk = e(nc.sbuf_tensor("zblk", [128, 2], F32))
        t1 = e(nc.sbuf_tensor("t1", [128, 1], F32))
        mv = e(nc.sbuf_tensor("mv", [128, 2], F32))
        st12 = e(nc.sbuf_tensor("st12", [128, 12], F32))
        veps = e(nc.sbuf_tensor("veps", [128, 1], F32))
        istd = e(nc.sbuf_tensor("istd", [128, 1], F32))
        amax = e(nc.sbuf_tensor("amax", [128, 1], F32))
        hsl = e(nc.sbuf_tensor("hsl", [128, 1], F32))
        ham16 = e(nc.sbuf_tensor("ham16", [128, 16], F32))
        acl = e(nc.sbuf_tensor("acl", [128, 1], F32))
        ah_s = e(nc.sbuf_tensor("ah_s", [128, 1], F32))
        MgB = e(nc.sbuf_tensor("MgB", [128, 1], F32))
        ZgB = e(nc.sbuf_tensor("ZgB", [128, 1], F32))
        rz = e(nc.sbuf_tensor("rz", [128, 1], F32))
        nbg = e(nc.sbuf_tensor("nbg", [128, 1], F32))
        mwg_b = e(nc.sbuf_tensor("mwg_b", [128, 1], F32))
        mwu_b = e(nc.sbuf_tensor("mwu_b", [128, 1], F32))
        mwd_b = e(nc.sbuf_tensor("mwd_b", [128, 1], F32))
        wthr6 = e(nc.sbuf_tensor("wthr6", [128, 6], F32))
        wacc3 = e(nc.sbuf_tensor("wacc3", [128, 3], F32))
        wred = e(nc.sbuf_tensor("wred", [128, 1], F32))
        w3r = e(nc.sbuf_tensor("w3r", [3, 128], F32))
        wsum3 = e(nc.sbuf_tensor("wsum3", [3, 1], F32))
        wquad = e(nc.sbuf_tensor("wquad", [3, 4], F32))
        strow = e(nc.sbuf_tensor("strow", [1, 128], F32))
        strow2 = e(nc.sbuf_tensor("strow2", [1, 128], F32))
        stg2 = e(nc.sbuf_tensor("stg2", [1, 16], F32))
        m8 = e(nc.sbuf_tensor("m8", [1, 8], F32))
        z8 = e(nc.sbuf_tensor("z8", [1, 8], F32))
        e8 = e(nc.sbuf_tensor("e8", [1, 8], F32))
        g1 = e(nc.sbuf_tensor("g1", [1, 1], F32))
        g2 = e(nc.sbuf_tensor("g2", [1, 1], F32))
        p_sb = e(nc.sbuf_tensor("p_sb", [128, 2 * KB], BF16))
        pT_sb = e(nc.sbuf_tensor("pT_sb", [128, 2 * KB], BF16))
        xb0 = e(nc.sbuf_tensor("xb0", [128, D], F32))
        xb1 = e(nc.sbuf_tensor("xb1", [128, D], F32))
        wstr0 = e(nc.sbuf_tensor("wstr0", [128, 1024], F32))
        wstr1 = e(nc.sbuf_tensor("wstr1", [128, 1024], F32))

        wneg = e(nc.sbuf_tensor("wneg", [128, 1024], F32))
        wtrn = e(nc.sbuf_tensor("wtrn", [128, 2048], BF16))
        wtmp = e(nc.sbuf_tensor("wtmp", [128, 2048], BF16))

        AR = 160 * 1024
        got = nc.bump_sbuf(AR, side="right")
        assert got is not None, "arena alloc failed"
        ar0 = got[0]

        def at(name, shape, dtype, off):
            return nc.alloc_sbuf_tensor_at(name, shape, dtype, offset=ar0 + off)

        QT = at("QT", [128, 8 * TPC], BF16, 0)
        xnTo = at("xnTo", [128, 8 * TPC], BF16, 32 * 1024)
        xnTc = at("xnTc", [128, 8 * TPC], BF16, 64 * 1024)
        w_cur = at("w_cur", [128, 8 * D], BF16, 96 * 1024)
        N_sb = at("N_sb", [128, NT * D], F32, 32 * 1024)
        kblk2 = [at("kblk0", [128, 8 * KB], BF16, 96 * 1024),
                 at("kblk1", [128, 8 * KB], BF16, 112 * 1024)]
        vblk2 = [at("vblk0", [128, 8 * KB], BF16, 128 * 1024),
                 at("vblk1", [128, 8 * KB], BF16, 144 * 1024)]
        tintT = at("tintT", [128, 8 * CHT], BF16, 0)
        xcs = at("xcs", [128, D], F32, 8 * 1024)
        iscr = at("iscr", [128, D], I32, 12 * 1024)
        tb_bf = at("tb_bf", [128, D], BF16, 16 * 1024)
        tb2_bf = at("tb2_bf", [128, D], BF16, 18 * 1024)
        scrA = at("scrA", [128, D], F32, 20 * 1024)
        scrB = at("scrB", [128, D], F32, 24 * 1024)
        g_bf = at("g_bf", [128, 1024], BF16, 28 * 1024)
        u_bf = at("u_bf", [128, 1024], BF16, 30 * 1024)
        bv_bc = at("bv_bc", [128, D], F32, 154 * 1024)
        kvstg0 = at("kvstg0", [128, 512], BF16, 158 * 1024)
        kvstg1 = at("kvstg1", [128, 512], BF16, 159 * 1024)
        wstr2 = at("wstr2", [128, 1024], F32, 112 * 1024)
        wstr3 = at("wstr3", [128, 1024], F32, 116 * 1024)

        wdT_sb = at("wdT_sb", [128, ICN * D], BF16, 32 * 1024)
        h_sb = at("h_sb", [128, CTL * INNER], BF16, 96 * 1024)
        hT_sb = at("hT_sb", [128, ICN * CHT], BF16, 96 * 1024)
        wgn = [at("wgn0", [128, 8 * 512], BF16, 128 * 1024),
               at("wgn1", [128, 8 * 512], BF16, 144 * 1024)]
        wun = [at("wun0", [128, 8 * 512], BF16, 136 * 1024),
               at("wun1", [128, 8 * 512], BF16, 152 * 1024)]

        ps_b0 = e(nc.psum_tensor("ps_b0", [128, 1024], F32))
        ps_b1 = e(nc.psum_tensor("ps_b1", [128, 1024], F32))
        ps_av = e(nc.psum_tensor("ps_av", [128, 1024], F32))
        ps_c = e(nc.psum_tensor("ps_c", [128, 2048], BF16))

        class Sem:
            def __init__(self, name):
                self.h = e(nc.semaphore(name))
                self.n = 0

        dS = Sem("dS")
        dA2 = [Sem("dA0"), Sem("dA1")]
        dG = Sem("dG")
        dW = Sem("dW")
        dKV = [Sem("dKV0"), Sem("dKV1")]
        dKe = [Sem("dKe0"), Sem("dKe1")]
        dTe = [Sem("dTe0"), Sem("dTe1")]
        dGa = [Sem("dGa0"), Sem("dGa1")]
        dGa2 = Sem("dGa2")
        dGa3 = Sem("dGa3")
        dXa = [Sem("dXa0"), Sem("dXa1")]
        dGb = [Sem("dGb0"), Sem("dGb1")]
        dX = [Sem("dX0"), Sem("dX1")]
        dWn = [Sem("dWn0"), Sem("dWn1")]
        dWd = Sem("dWd")
        dTi = [Sem("dTi0"), Sem("dTi1")]
        dTT = Sem("dTT")
        dPw = [Sem("dPw0"), Sem("dPw1")]
        dW8 = [Sem("dW80"), Sem("dW81")]
        dHT = [Sem("dHT0"), Sem("dHT1")]
        cc = Sem("cc")
        te = Sem("te")
        ve = Sem("ve")
        se = Sem("se")
        sw = Sem("sw")
        vw = Sem("vw")
        ge = Sem("ge")
        so = Sem("so")

        def inc(sem, instr, k=1):
            instr.then_inc(sem.h, k)
            sem.n += k

        block = e(nc.Block())

        @block.sync
        def _(sy):
            for dst, src in [
                (ident[:], ident_dr.ap()),
                (bq_sb[:], bass.AP(bq_d, 0, [[1, 128], [128, 8]])),
                (bk_sb[:], bass.AP(bk_d, 0, [[1, 128], [128, 8]])),
                (bv_bc[:], bass.AP(bv_d, 0, [[0, 128], [1, D]])),
                (cid_b[:], bass.AP(cid_d, 0, [[0, 128], [1, 1]])),
            ]:
                inc(dS, sy.dma_start(out=dst, in_=src,
                                     allow_slow_non_contiguous=True), 16)
        dS_setup = dS.n

        @block.vector
        def _(v):
            v.memset(eps_sb[:], EPS)
            v.drain()
            v.memset(stg2[:, :], 0.0)
            inc(ve, v.drain())

        @block.gpsimd
        def _(g):
            g.wait_ge(dS.h, dS_setup)
            g.iota(iscr[:], [[1, KB]], base=0, channel_multiplier=-1)
            inc(ge, g.drain())
        ge_iota = ge.n

        @block.vector
        def _(v):
            v.wait_ge(ge.h, ge_iota)
            v.tensor_scalar_mul(thr[:], cid_b[:], 128.0)
            v.drain()
            v.tensor_copy(mask[:], iscr[:])
            v.drain()
            v.tensor_single_scalar(mask[:], mask[:], thr[:, 0:1], ALU.is_le)
            v.drain()
            v.tensor_scalar(mask[:], mask[:], 1.0, -NEG, ALU.subtract, ALU.mult)
            inc(ve, v.drain())

        wins = [wg_d, wu_d, wd_d]
        wterns = [wgt_d, wut_d, wdt_d]

        def wtile_ap(wi, i):
            w = wins[wi]
            colchunks = w.shape[1] // 1024
            r, cch = divmod(i, colchunks)
            return w.ap()[r * 128:(r + 1) * 128, cch * 1024:(cch + 1) * 1024]

        def wtern_ap(wi, i):
            w = wterns[wi]
            colchunks = w.shape[1] // 1024
            r, cch = divmod(i, colchunks)
            return w.ap()[r * 128:(r + 1) * 128, cch * 1024:(cch + 1) * 1024]

        ln_srcs = [x_own_d, x_ctg_d]
        ln_dsts = [xn_own_d, xn_ctg_d]
        dS_a1_base = dS.n
        ve_a1_base = ve.n
        se_a1_base = se.n

        def ve_stats(k):
            return ve_a1_base + 2 * k + 1

        def ve_xn(k):
            return ve_a1_base + 2 * k + 2

        def se_sqrt(k):
            return se_a1_base + k + 1


        @block.sync
        def _(sy):
            sy.wait_ge(dS.h, dS_setup)

            def ld(k):
                si, tt = divmod(k, NT)
                xb = xb0 if (k % 2 == 0) else xb1
                inc(dXa[k % 2], sy.dma_start(
                    out=xb[:],
                    in_=ln_srcs[si].ap()[tt * 128:(tt + 1) * 128, :]), 16)
                assert dXa[k % 2].n == 16 * (k // 2 + 1)

            ld(0)
            ld(1)
            for k in range(32):
                sy.wait_ge(ve.h, ve_xn(k))
                if k + 2 < 32:
                    ld(k + 2)
                si, tt = divmod(k, NT)
                tbk = tb_bf if k % 2 == 0 else tb2_bf
                inc(dTi[k % 2], sy.dma_start(
                    out=ln_dsts[si].ap()[tt * 128:(tt + 1) * 128, :],
                    in_=tbk[:]), 16)
                assert dTi[k % 2].n == 16 * (k // 2 + 1)
        a1_st_done = (dTi[0].n, dTi[1].n)
        dS_a1_done = dS.n

        @block.vector
        def _(v):
            wbufs = [wstr0, wstr1, wstr2, wstr3]
            wsems = [dGa[0], dGa[1], dGa2, dGa3]
            v.memset(wacc3[:], 0.0)
            v.drain()
            for k in range(32):
                xb = xb0 if (k % 2 == 0) else xb1
                tbk = tb_bf if k % 2 == 0 else tb2_bf
                v.wait_ge(dXa[k % 2].h, 16 * (k // 2 + 1))
                v.bn_stats(st12[:, 0:6], xb[:, 0:512])
                v.bn_stats(st12[:, 6:12], xb[:, 512:1024])
                v.drain()
                v.bn_aggr(mv[:], st12[:])
                inc(ve, v.drain())
                v.wait_ge(se.h, se_sqrt(k))
                if k >= 2:
                    v.wait_ge(dTi[k % 2].h, 16 * (k // 2))
                v.reciprocal(istd[:], veps[:])
                v.drain()
                v.tensor_scalar(xcs[:], xb[:], mv[:, 0:1], istd[:, 0:1],
                                ALU.subtract, ALU.mult)
                v.drain()
                v.tensor_copy(tbk[:], xcs[:])
                inc(ve, v.drain())
                for i3 in range(3):
                    i = 3 * k + i3
                    wi, wt = divmod(i, NWT)
                    v.wait_ge(wsems[i % 4].h, 16 * (i // 4 + 1))
                    v.reduce_sum(wred[:, 0:1], wbufs[i % 4][:], axis=AX.X,
                                 apply_absolute_value=True)
                    v.drain()
                    v.tensor_add(wacc3[:, wi:wi + 1], wred[:, 0:1],
                                 wacc3[:, wi:wi + 1])
                    inc(sw, v.drain())
        assert ve.n == ve_a1_base + 64
        assert sw.n == 96

        @block.scalar
        def _(s):
            wbufs = [wstr0, wstr1, wstr2, wstr3]
            wsems = [dGa[0], dGa[1], dGa2, dGa3]

            def ld_even(iq):
                if iq < 96 and iq % 2 == 0:
                    if iq >= 4:
                        s.wait_ge(sw.h, iq - 3)
                    inc(wsems[iq % 4], s.dma_start(
                        out=wbufs[iq % 4][:], in_=wtile_ap(*divmod(iq, NWT))),
                        16)

            ld_even(0)
            ld_even(2)
            for k in range(32):
                s.wait_ge(ve.h, ve_stats(k))
                s.activation(veps[:], mv[:, 1:2], AF.Sqrt, bias=eps_sb[:, 0:1])
                inc(se, s.drain())
                for iq in range(3 * k + 4, 3 * k + 7):
                    ld_even(iq)
        assert se.n == se_a1_base + 32
        assert dGa[0].n + dGa2.n == 16 * 48

        @block.sync
        def _(sy):
            sy.wait_ge(sw.h, 96)
            sy.wait_ge(dS.h, dS_a1_done)
            inc(dS, sy.dma_start(out=wcol3_d.ap()[:], in_=wacc3[:, :]), 16)
            sy.wait_ge(dS.h, dS.n)
            inc(dS, sy.dma_start(out=w3r[:, :],
                                 in_=bass.AP(wcol3_d, 0, [[1, 3], [3, 128]]),
                                 allow_slow_non_contiguous=True), 16)
        dS_w3r = dS.n

        @block.vector
        def _(v):
            v.wait_ge(dS.h, dS_w3r)
            v.memset(wquad[:, 3:4], 0.0)
            v.drain()
            v.reduce_sum(wsum3[:, 0:1], w3r[:, :], axis=AX.X)
            v.drain()
            v.tensor_scalar_mul(wsum3[:, 0:1], wsum3[:, 0:1], 1.0 / (INNER * D))
            v.drain()
            v.tensor_scalar_max(wquad[:, 0:1], wsum3[:, 0:1], 1e-5)
            v.drain()
            v.tensor_scalar_mul(wquad[:, 1:2], wquad[:, 0:1], 0.5)
            v.drain()
            v.tensor_scalar_mul(wquad[:, 2:3], wquad[:, 0:1], -0.5)
            inc(ve, v.drain())
        ve_wq = ve.n

        @block.sync
        def _(sy):
            sy.wait_ge(ve.h, ve_wq)
            inc(dS, sy.dma_start(out=wsc12_d.ap()[0:12], in_=wquad[:, :]), 16)
            sy.wait_ge(dS.h, dS.n)
            inc(dS, sy.dma_start(out=mwg_b[:],
                                 in_=bass.AP(wsc12_d, 0, [[0, 128], [1, 1]])), 16)
            inc(dS, sy.dma_start(out=mwu_b[:],
                                 in_=bass.AP(wsc12_d, 4, [[0, 128], [1, 1]])), 16)
            inc(dS, sy.dma_start(out=mwd_b[:],
                                 in_=bass.AP(wsc12_d, 8, [[0, 128], [1, 1]])), 16)
            for wi in range(3):
                inc(dS, sy.dma_start(
                    out=wthr6[:, 2 * wi:2 * wi + 2],
                    in_=bass.AP(wsc12_d, 4 * wi + 1, [[0, 128], [1, 2]])), 16)
        dS_means = dS.n

        @block.sync
        def _(sy):
            sy.wait_ge(dS.h, dS_means)
            sy.wait_ge(dTi[0].h, a1_st_done[0])
            sy.wait_ge(dTi[1].h, a1_st_done[1])
            for dc in range(DC):
                inc(dS, sy.dma_start_transpose(
                    out=xnTo[:, dc * TPC:(dc + 1) * TPC],
                    in_=xn_own_d.ap()[:, dc * 128:(dc + 1) * 128]), 16)
            for dc in range(DC):
                inc(dS, sy.dma_start_transpose(
                    out=xnTc[:, dc * TPC:(dc + 1) * TPC],
                    in_=xn_ctg_d.ap()[:, dc * 128:(dc + 1) * 128]), 16)
        dS_xnT = dS.n

        te_a2_base = te.n
        ve_a2_base = ve.n

        def te_qg(gi):
            return te_a2_base + gi + 1

        def te_kg(gi):
            return te_a2_base + 32 + gi + 1

        def te_vg(gi):
            return te_a2_base + 64 + gi + 1

        def ve_qe(gi):
            return ve_a2_base + gi + 1

        def ve_ke(gi):
            return ve_a2_base + 32 + gi + 1

        def ve_vee(gi):
            return ve_a2_base + 64 + gi + 1

        aps = [ps_b0[:, 0:512], ps_b0[:, 512:1024]]

        dW_wq = 16 * DC
        dW_wk = 32 * DC
        dW_wv = 48 * DC

        @block.tensor
        def _(t):
            t.wait_ge(dS.h, dS_xnT)
            t.wait_ge(dW.h, dW_wq)
            for gi in range(32):
                ec, cc4 = divmod(gi, 4)
                ps = aps[gi % 2]
                if gi >= 2:
                    t.wait_ge(ve.h, ve_qe(gi - 2))
                for dc in range(DC):
                    mm = t.matmul(ps,
                                  w_cur[:, dc * D + ec * 128: dc * D + (ec + 1) * 128],
                                  xnTo[:, dc * TPC + cc4 * 512: dc * TPC + (cc4 + 1) * 512],
                                  start=(dc == 0), stop=(dc == DC - 1))
                inc(te, mm)
            t.wait_ge(dW.h, dW_wk)
            for gi in range(32):
                ec, cc4 = divmod(gi, 4)
                ps = aps[gi % 2]
                t.wait_ge(ve.h, ve_ke(gi - 2))
                for dc in range(DC):
                    mm = t.matmul(ps,
                                  w_cur[:, dc * D + ec * 128: dc * D + (ec + 1) * 128],
                                  xnTc[:, dc * TPC + cc4 * 512: dc * TPC + (cc4 + 1) * 512],
                                  start=(dc == 0), stop=(dc == DC - 1))
                inc(te, mm)
            t.wait_ge(dW.h, dW_wv)
            for gi in range(32):
                tt2, cc2 = divmod(gi, 2)
                ps = aps[gi % 2]
                t.wait_ge(ve.h, ve_vee(gi - 2))
                for dc in range(DC):
                    mm = t.matmul(ps,
                                  xnTc[:, dc * TPC + tt2 * 128: dc * TPC + (tt2 + 1) * 128],
                                  w_cur[:, dc * D + cc2 * 512: dc * D + (cc2 + 1) * 512],
                                  start=(dc == 0), stop=(dc == DC - 1))
                inc(te, mm)
        assert te.n == te_a2_base + 96

        def dKe_kexp(gi):
            return 16 * (gi // 2 + 1)

        def dKe_vexp(gi):
            return 16 * (16 + gi // 2 + 1)

        def dTe_tern(i):
            return 16 * (i // 2 + 1)

        @block.vector
        def _(v):
            for gi in range(32):
                ec, cc4 = divmod(gi, 4)
                ps = aps[gi % 2]
                v.wait_ge(te.h, te_qg(gi))
                v.tensor_scalar_add(QT[:, ec * TPC + cc4 * 512: ec * TPC + (cc4 + 1) * 512],
                                    ps, bq_sb[:, ec:ec + 1])
                inc(ve, v.drain())
            for gi in range(32):
                ec, cc4 = divmod(gi, 4)
                ps = aps[gi % 2]
                stg = kvstg0 if gi % 2 == 0 else kvstg1
                v.wait_ge(te.h, te_kg(gi))
                if gi >= 2:
                    v.wait_ge(dKe[gi % 2].h, dKe_kexp(gi - 2))
                v.tensor_scalar_add(stg[:], ps, bk_sb[:, ec:ec + 1])
                inc(ve, v.drain())
            for gi in range(32):
                tt2, cc2 = divmod(gi, 2)
                ps = aps[gi % 2]
                stg = kvstg0 if gi % 2 == 0 else kvstg1
                v.wait_ge(te.h, te_vg(gi))
                if gi >= 2:
                    v.wait_ge(dKe[gi % 2].h, dKe_vexp(gi - 2))
                else:
                    v.wait_ge(dKe[gi % 2].h, 16 * 16)
                v.tensor_add(stg[:], ps, bv_bc[:, cc2 * 512:(cc2 + 1) * 512])
                inc(ve, v.drain())
        assert ve.n == ve_a2_base + 96

        @block.sync
        def _(sy):
            sy.wait_ge(dS.h, dS_xnT)
            for gi in range(32):
                ec, cc4 = divmod(gi, 4)
                stg = kvstg0 if gi % 2 == 0 else kvstg1
                sy.wait_ge(ve.h, ve_ke(gi))
                inc(dKe[gi % 2], sy.dma_start(
                    out=bass.AP(ccKVi, ec * 128 * TPC + cc4 * 512,
                                [[TPC, 128], [1, 512]]),
                    in_=stg[:]), 16)
            for gi in range(32):
                tt2, cc2 = divmod(gi, 2)
                stg = kvstg0 if gi % 2 == 0 else kvstg1
                sy.wait_ge(ve.h, ve_vee(gi))
                inc(dKe[gi % 2], sy.dma_start(
                    out=bass.AP(ccKVi, KVSZ + tt2 * 128 * D + cc2 * 512,
                                [[D, 128], [1, 512]]),
                    in_=stg[:]), 16)
            assert dKe[0].n == 16 * 32 and dKe[1].n == 16 * 32

        @block.gpsimd
        def _(g):
            for dc in range(DC):
                inc(dW, g.dma_start(out=w_cur[:, dc * D:(dc + 1) * D],
                                    in_=wq_d.ap()[dc * 128:(dc + 1) * 128, :]), 16)
            for i in range(1, 96, 2):
                wi, wt = divmod(i, NWT)
                buf = wstr1 if i % 4 == 1 else wstr3
                sem = dGa[1] if i % 4 == 1 else dGa3
                if i >= 4:
                    g.wait_ge(sw.h, i - 3)
                inc(sem, g.dma_start(out=buf[:], in_=wtile_ap(wi, wt)), 16)
            g.wait_ge(te.h, te_qg(31))
            for dc in range(DC):
                inc(dW, g.dma_start(out=w_cur[:, dc * D:(dc + 1) * D],
                                    in_=wk_d.ap()[dc * 128:(dc + 1) * 128, :]), 16)
            g.wait_ge(te.h, te_kg(31))
            for dc in range(DC):
                inc(dW, g.dma_start(out=w_cur[:, dc * D:(dc + 1) * D],
                                    in_=wv_d.ap()[dc * 128:(dc + 1) * 128, :]), 16)
            assert dW.n == dW_wv
            g.wait_ge(dKe[0].h, 16 * 32)
            g.wait_ge(dKe[1].h, 16 * 32)
            inc(cc, g.collective_compute(
                "AllGather", ALU.bypass, replica_groups=[list(range(NCORES))],
                ins=[ccKVi.ap().opt()], outs=[ccKVo.ap().opt()]))
            for i in range(96):
                wi, wt = divmod(i, NWT)
                buf = wstr0 if i % 2 == 0 else wstr1
                if i == 0:
                    g.wait_ge(sw.h, 96)
                if i >= 2:
                    g.wait_ge(vw.h, i - 1)
                inc(dGb[i % 2], g.dma_start(out=buf[:], in_=wtile_ap(wi, wt)), 16)
        cc_kv = cc.n

        teC0, veC0, seC0 = te.n, ve.n, se.n

        it_list = []
        for b in range(NB):
            for j in range(b, NB):
                it_list.append((b, j))
        assert len(it_list) == N_IT

        def cum(b):
            return sum(NB - i for i in range(b))

        te_S = {}
        te_TR = {}
        te_AV = {}
        tcnt = teC0
        order = []
        for k in range(min(2, N_IT)):
            order.append(("S", k))
        for k in range(N_IT):
            order.append(("TR", k))
            order.append(("AV", k))
            if k + 2 < N_IT:
                order.append(("S", k + 2))
        for kind, k in order:
            tcnt += 1
            {"S": te_S, "TR": te_TR, "AV": te_AV}[kind][k] = tcnt

        ve1 = {}
        ve2 = {}
        vcnt = veC0
        for k in range(min(2, N_IT)):
            vcnt += 1
            ve1[k] = vcnt
        for k in range(N_IT):
            vcnt += 1
            ve2[k] = vcnt
            if k + 2 < N_IT:
                vcnt += 1
                ve1[k + 2] = vcnt
        se1 = {}
        se2 = {}
        scnt = seC0
        for k in range(N_IT):
            if k >= 1:
                scnt += 1
                se2[k - 1] = scnt
            scnt += 1
            se1[k] = scnt
        scnt += 1
        se2[N_IT - 1] = scnt

        wt_pairs = []
        for wi, (src, dst, prow) in enumerate([
                (wgt_d, wgT_d, 1024), (wut_d, wuT_d, 1024), (wdt_d, wdT_d, 1024)]):
            nchk = src.shape[1] // 128
            npc = src.shape[0] // prow
            colchunks = src.shape[1] // 1024
            for ck in range(nchk):
                for pc in range(npc):
                    last_tile = wi * NWT + ((pc + 1) * prow // 128) * colchunks - 1
                    g0 = 16 * (last_tile // 2 + 1)
                    g1v = 16 * ((last_tile + 1) // 2)
                    wt_pairs.append((src, dst, prow, ck, pc, g0, g1v))
        NPAIR = len(wt_pairs)
        assert NPAIR == 32 + 32 + 32
        wt_wr_mark = {}
        wt_rd_mark = {}
        def pair_L(pr):
            g0p = pr[5] // 16
            return 2 * g0p - 1
        order_p = sorted(range(NPAIR), key=lambda p: (pair_L(wt_pairs[p]), p))
        pair_at_iter = {}
        it = 0
        for p in order_p:
            it = max(pair_L(wt_pairs[p]) + 2, it + 1)
            pair_at_iter[it] = p
        assert max(pair_at_iter) < N_IT

        def sec_of_beat(i):
            b = 0
            while b + 1 < NB and cum(b) - 1 <= i:
                b += 1
            return b
        sec_stores = {b: [] for b in range(NB)}
        for i in range(96):
            sec_stores[sec_of_beat(i)].append(i)
        sec_reads = {b: [] for b in range(NB + 2)}
        for p in sorted(range(NPAIR), key=lambda p: (pair_L(wt_pairs[p]), p)):
            sec_reads[sec_of_beat(pair_L(wt_pairs[p]))].append(p)
        pend_wr2 = []

        def emit_pair_read2(sy, kp):
            seq = len(wt_rd_mark)
            while pend_wr2 and pend_wr2[0][0] <= seq - 2:
                emit_pair_write2(sy)
            src, dst, prow, ck, pc, g0, g1v = wt_pairs[kp]
            pr2 = seq % 2
            half = wtmp[:, 0:prow] if pr2 == 0 else wtmp[:, 1024:1024 + prow]
            sy.wait_ge(dTe[0].h, g0)
            if g1v > 0:
                sy.wait_ge(dTe[1].h, g1v)
            if seq >= 2:
                sy.wait_ge(dA2[pr2].h, wt_wr_mark[seq - 2])
            inc(dA2[pr2], sy.dma_start_transpose(
                out=half,
                in_=src.ap()[pc * prow:(pc + 1) * prow,
                             ck * 128:(ck + 1) * 128]), 16)
            wt_rd_mark[seq] = dA2[pr2].n
            pend_wr2.append((seq, kp))

        def emit_pair_write2(sy):
            seq, kp = pend_wr2.pop(0)
            src, dst, prow, ck, pc, g0, g1v = wt_pairs[kp]
            pr2 = seq % 2
            half = wtmp[:, 0:prow] if pr2 == 0 else wtmp[:, 1024:1024 + prow]
            sy.wait_ge(dA2[pr2].h, wt_rd_mark[seq])
            inc(dA2[pr2], sy.dma_start(
                out=dst.ap()[ck * 128:(ck + 1) * 128,
                             pc * prow:(pc + 1) * prow],
                in_=half), 16)
            wt_wr_mark[seq] = dA2[pr2].n

        @block.sync
        def _(sy):
            sy.wait_ge(cc.h, cc_kv)
            for b in range(NB):
                if b >= 2:
                    sy.wait_ge(te.h, te_AV[cum(b - 1) - 1])
                rk, co = divmod(b, 2)
                kb = kblk2[b % 2]
                vb = vblk2[b % 2]
                for dc in range(DC):
                    inc(dKV[b % 2], sy.dma_start(
                        out=kb[:, dc * KB:(dc + 1) * KB],
                        in_=bass.AP(ccKVo,
                                    rk * 2 * KVSZ + dc * 128 * TPC + co * KB,
                                    [[TPC, 128], [1, KB]])), 16)
                for kc in range(DC):
                    inc(dKV[b % 2], sy.dma_start(
                        out=vb[:, kc * KB:(kc + 1) * KB],
                        in_=bass.AP(ccKVo,
                                    rk * 2 * KVSZ + KVSZ + (co * KB + kc * 128) * D,
                                    [[D, 128], [1, KB]])), 16)
                while pend_wr2:
                    emit_pair_write2(sy)
                for p in sec_reads.get(b, []):
                    emit_pair_read2(sy, p)
            for b in range(NB, NB + 2):
                for p in sec_reads.get(b, []):
                    emit_pair_read2(sy, p)
            while pend_wr2:
                emit_pair_write2(sy)
        assert dKV[0].n == (NB // 2) * 256 and dKV[1].n == (NB // 2) * 256
        assert len(wt_wr_mark) == NPAIR
        dA_wT = (dA2[0].n, dA2[1].n)

        def emit_S(t, k):
            b, j = it_list[k]
            t.wait_ge(dKV[b % 2].h, 256 * (b // 2 + 1))
            if k >= 2:
                t.wait_ge(se.h, se1[k - 2])
            sp = ps_b0 if k % 2 == 0 else ps_b1
            kb = kblk2[b % 2]
            for dc in range(DC):
                lhs = QT[:, dc * TPC + j * 128: dc * TPC + (j + 1) * 128]
                t.matmul(sp[:, 0:512], lhs,
                         kb[:, dc * KB: dc * KB + 512],
                         start=(dc == 0), stop=(dc == DC - 1))
                mm = t.matmul(sp[:, 512:1024], lhs,
                              kb[:, dc * KB + 512: (dc + 1) * KB],
                              start=(dc == 0), stop=(dc == DC - 1))
            inc(te, mm)
            assert te.n == te_S[k]

        def emit_TR(t, k):
            t.wait_ge(se.h, se1[k])
            off = (k % 2) * KB
            for kc in range(DC):
                tr = t.transpose(ps_c[:, off + kc * 128: off + (kc + 1) * 128],
                                 p_sb[:, off + kc * 128: off + (kc + 1) * 128],
                                 ident[:])
            inc(te, tr)
            assert te.n == te_TR[k]

        def emit_AV(t, k):
            b, j = it_list[k]
            t.wait_ge(se.h, se2[k])
            if k >= 1:
                t.wait_ge(ve.h, ve2[k - 1])
            off = (k % 2) * KB
            vb = vblk2[b % 2]
            for kc in range(DC):
                lhs = pT_sb[:, off + kc * 128: off + (kc + 1) * 128]
                t.matmul(ps_av[:, 0:512], lhs,
                         vb[:, kc * KB: kc * KB + 512],
                         start=(kc == 0), stop=(kc == DC - 1))
                mm = t.matmul(ps_av[:, 512:1024], lhs,
                              vb[:, kc * KB + 512: (kc + 1) * KB],
                              start=(kc == 0), stop=(kc == DC - 1))
            inc(te, mm)
            assert te.n == te_AV[k]

        @block.tensor
        def _(t):
            for kind, k in order:
                if kind == "S":
                    emit_S(t, k)
                elif kind == "TR":
                    emit_TR(t, k)
                else:
                    emit_AV(t, k)
        assert te.n == tcnt

        def emit_part1(v, k):
            b, j = it_list[k]
            first = (b == 0)
            sp = ps_b0 if k % 2 == 0 else ps_b1
            pr = k % 2
            v.wait_ge(te.h, te_S[k])
            if j == b:
                v.tensor_add(sp[:], sp[:], mask[:])
                v.drain()
            v.reduce_max(rowmax[:, 0:1], sp[:], axis=AX.X)
            v.drain()
            if first:
                v.tensor_copy(m_sb[:, j:j + 1], rowmax[:, 0:1])
                v.drain()
            else:
                v.tensor_scalar(d2[:, pr:pr + 1], m_sb[:, j:j + 1],
                                rowmax[:, 0:1], 0.0, ALU.subtract, ALU.min)
                v.drain()
                v.tensor_single_scalar(m_sb[:, j:j + 1], m_sb[:, j:j + 1],
                                       rowmax[:, 0:1], ALU.max)
                v.drain()
            v.tensor_scalar_mul(nbias[:, pr:pr + 1], m_sb[:, j:j + 1], -SCL)
            inc(ve, v.drain())
            assert ve.n == ve1[k]

        def emit_part2(v, k):
            b, j = it_list[k]
            first = (b == 0)
            pr = k % 2
            v.wait_ge(te.h, te_AV[k])
            if first:
                v.tensor_copy(N_sb[:, j * D:(j + 1) * D], ps_av[:])
                v.drain()
                v.tensor_copy(z_sb[:, j:j + 1], zblk[:, pr:pr + 1])
            else:
                v.tensor_scalar_min(cfac[:, pr:pr + 1], cfac_a[:, pr:pr + 1], 1.0)
                v.drain()
                v.scalar_tensor_tensor(N_sb[:, j * D:(j + 1) * D],
                                       N_sb[:, j * D:(j + 1) * D],
                                       cfac[:, pr:pr + 1],
                                       ps_av[:], ALU.mult, ALU.add)
                v.drain()
                v.scalar_tensor_tensor(z_sb[:, j:j + 1], z_sb[:, j:j + 1],
                                       cfac[:, pr:pr + 1],
                                       zblk[:, pr:pr + 1], ALU.mult, ALU.add)
            inc(ve, v.drain())
            assert ve.n == ve2[k]

        def emit_tern(v, k):
            if k < 96:
                i = k
                wi, wt = divmod(i, NWT)
                buf = wstr0 if i % 2 == 0 else wstr1
                trn = wtrn[:, 0:1024] if i % 2 == 0 else wtrn[:, 1024:2048]
                if i == 0:
                    v.wait_ge(dS.h, dS_means)
                v.wait_ge(dGb[i % 2].h, 16 * (i // 2 + 1))
                if i >= 2:
                    v.wait_ge(dTe[i % 2].h, dTe_tern(i - 2))
                v.tensor_single_scalar(wneg[:], buf[:],
                                       wthr6[:, 2 * wi + 1:2 * wi + 2], ALU.is_lt)
                v.drain()
                v.scalar_tensor_tensor(trn, buf[:],
                                       wthr6[:, 2 * wi:2 * wi + 1], wneg[:],
                                       ALU.is_gt, ALU.subtract)
                inc(vw, v.drain())

        @block.vector
        def _(v):
            for k in range(min(2, N_IT)):
                emit_part1(v, k)
            for k in range(N_IT):
                emit_part2(v, k)
                if k + 2 < N_IT:
                    emit_part1(v, k + 2)
                emit_tern(v, k)
        assert ve.n == vcnt
        assert vw.n == 96

        def emit_expgrp(s, k):
            b, j = it_list[k]
            first = (b == 0)
            sp = ps_b0 if k % 2 == 0 else ps_b1
            pr = k % 2
            s.wait_ge(ve.h, ve1[k])
            if not first:
                s.activation(cfac_a[:, pr:pr + 1], d2[:, pr:pr + 1],
                             AF.Exp, scale=SCL)
                s.drain()
            s.activation(p_sb[:, pr * KB:(pr + 1) * KB], sp[:], AF.Exp,
                         bias=nbias[:, pr:pr + 1], scale=SCL,
                         accum_out=zblk[:, pr:pr + 1])
            inc(se, s.drain())
            assert se.n == se1[k]

        def emit_ptcopy(s, k):
            pr = k % 2
            s.wait_ge(te.h, te_TR[k])
            s.activation(pT_sb[:, pr * KB:(pr + 1) * KB],
                         ps_c[:, pr * KB:(pr + 1) * KB], AF.Copy)
            inc(se, s.drain())
            assert se.n == se2[k]

        pend_wr = []

        def emit_pair_read(s, kp):
            src, dst, prow, ck, pc, g0, g1v = wt_pairs[kp]
            seq = len(wt_rd_mark)
            pr2 = seq % 2
            half = wtmp[:, 0:prow] if pr2 == 0 else wtmp[:, 1024:1024 + prow]
            s.wait_ge(dTe[0].h, g0)
            if g1v > 0:
                s.wait_ge(dTe[1].h, g1v)
            if seq >= 2:
                s.wait_ge(dA2[pr2].h, wt_wr_mark[seq - 2])
            inc(dA2[pr2], s.dma_start_transpose(
                out=half,
                in_=src.ap()[pc * prow:(pc + 1) * prow,
                             ck * 128:(ck + 1) * 128]), 16)
            wt_rd_mark[seq] = dA2[pr2].n
            pend_wr.append((seq, kp))

        def emit_pair_write(s):
            seq, kp = pend_wr.pop(0)
            src, dst, prow, ck, pc, g0, g1v = wt_pairs[kp]
            pr2 = seq % 2
            half = wtmp[:, 0:prow] if pr2 == 0 else wtmp[:, 1024:1024 + prow]
            s.wait_ge(dA2[pr2].h, wt_rd_mark[seq])
            inc(dA2[pr2], s.dma_start(
                out=dst.ap()[ck * 128:(ck + 1) * 128,
                             pc * prow:(pc + 1) * prow],
                in_=half), 16)
            wt_wr_mark[seq] = dA2[pr2].n

        @block.scalar
        def _(s):
            for k in range(N_IT):
                if 2 <= k < 98:
                    i = k - 2
                    trn = wtrn[:, 0:1024] if i % 2 == 0 else wtrn[:, 1024:2048]
                    wi, wt = divmod(i, NWT)
                    s.wait_ge(vw.h, i + 1)
                    inc(dTe[i % 2], s.dma_start(out=wtern_ap(wi, wt),
                                                in_=trn), 16)
                if k >= 1:
                    emit_ptcopy(s, k - 1)
                emit_expgrp(s, k)
            emit_ptcopy(s, N_IT - 1)
        assert se.n == scnt
        assert dTe[0].n == 16 * 48 and dTe[1].n == 16 * 48
        teC_end, veC_end, seC_end = te.n, ve.n, se.n

        @block.vector
        def _(v):
            v.wait_ge(ve.h, veC_end)
            v.reduce_max(t1[:, 0:1], m_sb[:], axis=AX.X)
            inc(ve, v.drain())
        vS1 = ve.n

        @block.sync
        def _(sy):
            sy.wait_ge(ve.h, vS1)
            inc(dS, sy.dma_start(out=col_d.ap()[:], in_=t1[:, 0:1]), 16)
            sy.wait_ge(dS.h, dS.n)
            inc(dS, sy.dma_start(out=strow[:, :], in_=col_d.ap()[:]), 16)
        dS_S1 = dS.n

        @block.vector
        def _(v):
            v.wait_ge(dS.h, dS_S1)
            v.reduce_max(g1[:, 0:1], strow[:, :], axis=AX.X)
            inc(ve, v.drain())
        vS2 = ve.n

        @block.sync
        def _(sy):
            sy.wait_ge(ve.h, vS2)
            inc(dS, sy.dma_start(out=sc2_d.ap()[0:1], in_=g1[:, 0:1]), 16)
            sy.wait_ge(dS.h, dS.n)
            inc(dS, sy.dma_start(
                out=MgB[:], in_=bass.AP(sc2_d, 0, [[0, 128], [1, 1]])), 16)
        dS_S2 = dS.n

        @block.vector
        def _(v):
            v.wait_ge(dS.h, dS_S2)
            v.tensor_scalar_mul(nbg[:, 0:1], MgB[:, 0:1], -SCL)
            inc(ve, v.drain())
        vS3 = ve.n

        sS1 = seC_end + 1

        @block.vector
        def _(v):
            v.wait_ge(se.h, sS1)
            v.tensor_mul(ef16[:], ef16[:], z_sb[:])
            v.drain()
            v.reduce_sum(t1[:, 0:1], ef16[:], axis=AX.X)
            inc(ve, v.drain())
        vS4 = ve.n

        @block.sync
        def _(sy):
            sy.wait_ge(ve.h, vS4)
            inc(dS, sy.dma_start(out=col_d.ap()[:], in_=t1[:, 0:1]), 16)
            sy.wait_ge(dS.h, dS.n)
            inc(dS, sy.dma_start(out=strow[:, :], in_=col_d.ap()[:]), 16)
        dS_S3 = dS.n

        @block.vector
        def _(v):
            v.wait_ge(dS.h, dS_S3)
            v.reduce_sum(g2[:, 0:1], strow[:, :], axis=AX.X)
            v.drain()
            v.tensor_copy(stg2[:, 0:1], g1[:, 0:1])
            v.drain()
            v.tensor_copy(stg2[:, 1:2], g2[:, 0:1])
            inc(ve, v.drain())
        vS5 = ve.n

        @block.sync
        def _(sy):
            sy.wait_ge(ve.h, vS5)
            inc(dS, sy.dma_start(out=sti.ap()[:, :], in_=stg2[:, :]), 16)
        dS_S4 = dS.n

        @block.gpsimd
        def _(g):
            g.wait_ge(dS.h, dS_S4)
            inc(cc, g.collective_compute(
                "AllGather", ALU.bypass, replica_groups=[list(range(NCORES))],
                ins=[sti.ap().opt()], outs=[sto_d.ap().opt()]))
            g.wait_ge(cc.h, cc.n)
            inc(dG, g.dma_start(
                out=strow2[:, :],
                in_=bass.AP(sto_d, 0, [[128, 1], [1, 128]])), 16)
        dG_S = dG.n

        @block.vector
        def _(v):
            v.wait_ge(dG.h, dG_S)
            v.tensor_copy(m8[:, :], bass.AP(strow2, 0, [[128, 1], [16, 8]]))
            v.drain()
            v.tensor_copy(z8[:, :], bass.AP(strow2, 1, [[128, 1], [16, 8]]))
            v.drain()
            v.reduce_max(g1[:, 0:1], m8[:, :], axis=AX.X)
            v.drain()
            v.tensor_scalar_mul(g2[:, 0:1], g1[:, 0:1], -SCL)
            inc(ve, v.drain())
        vS6 = ve.n

        sS2 = seC_end + 2

        @block.vector
        def _(v):
            v.wait_ge(se.h, sS2)
            v.tensor_mul(e8[:, :], e8[:, :], z8[:, :])
            v.drain()
            v.reduce_sum(g2[:, 0:1], e8[:, :], axis=AX.X)
            v.drain()
            v.tensor_copy(stg2[:, 2:3], g1[:, 0:1])
            v.drain()
            v.tensor_copy(stg2[:, 3:4], g2[:, 0:1])
            inc(ve, v.drain())
        vS7 = ve.n

        @block.sync
        def _(sy):
            sy.wait_ge(ve.h, vS7)
            inc(dS, sy.dma_start(out=sc2_d.ap()[2:4], in_=stg2[:, 2:4]), 16)
            sy.wait_ge(dS.h, dS.n)
            inc(dS, sy.dma_start(
                out=MgB[:], in_=bass.AP(sc2_d, 2, [[0, 128], [1, 1]])), 16)
            inc(dS, sy.dma_start(
                out=ZgB[:], in_=bass.AP(sc2_d, 3, [[0, 128], [1, 1]])), 16)
        dS_S5 = dS.n

        @block.vector
        def _(v):
            v.wait_ge(dS.h, dS_S5)
            v.reciprocal(rz[:, 0:1], ZgB[:, 0:1])
            v.drain()
            v.tensor_scalar_mul(nbg[:, 0:1], MgB[:, 0:1], -SCL)
            inc(ve, v.drain())
        vS8 = ve.n

        sS3 = seC_end + 3

        @block.vector
        def _(v):
            v.wait_ge(se.h, sS3)
            v.tensor_scalar_mul(f_sb[:], ef16[:], rz[:, 0:1])
            inc(ve, v.drain())
        vS_f = ve.n

        d_v1 = {}
        d_v2 = {}
        d_done = {}
        d_tint = {}
        seD0 = seC_end + 3
        for j in range(NT):
            xb = xb0 if j % 2 == 0 else xb1

            @block.sync
            def _(sy, j=j, xb=xb):
                if j >= 2:
                    sy.wait_ge(ve.h, d_v2[j - 2])
                inc(dX[j % 2], sy.dma_start(
                    out=xb[:], in_=x_own_d.ap()[j * 128:(j + 1) * 128, :]), 16)
            _dj_x = (j % 2, dX[j % 2].n)

            @block.vector
            def _(v, j=j, djx=_dj_x):
                v.wait_ge(ve.h, vS_f)
                v.wait_ge(dX[djx[0]].h, djx[1])
                v.tensor_scalar_mul(scrA[:], N_sb[:, j * D:(j + 1) * D],
                                    f_sb[:, j:j + 1])
                v.drain()
                v.bn_stats(st12[:, 0:6], scrA[:, 0:512])
                v.bn_stats(st12[:, 6:12], scrA[:, 512:1024])
                v.drain()
                v.bn_aggr(mv[:], st12[:])
                inc(ve, v.drain())
            _vj1 = ve.n
            d_v1[j] = _vj1
            _sj1 = seD0 + 2 * j + 1

            @block.vector
            def _(v, j=j, xb=xb, s1=_sj1):
                v.wait_ge(se.h, s1)
                v.reciprocal(istd[:], veps[:])
                v.drain()
                v.tensor_scalar(scrB[:], scrA[:], mv[:, 0:1], istd[:, 0:1],
                                ALU.subtract, ALU.mult)
                v.drain()
                v.tensor_add(scrA[:], xb[:], scrB[:])
                v.drain()
                v.bn_stats(st12[:, 0:6], scrA[:, 0:512])
                v.bn_stats(st12[:, 6:12], scrA[:, 512:1024])
                v.drain()
                v.bn_aggr(mv[:], st12[:])
                inc(ve, v.drain())
            _vj2 = ve.n
            d_v2[j] = _vj2

            @block.sync
            def _(sy, j=j, v2=_vj2):
                sy.wait_ge(ve.h, v2)
                inc(dS, sy.dma_start(
                    out=xres_d.ap()[j * 128:(j + 1) * 128, :], in_=scrA[:]), 16)
            _dj_xr = dS.n
            _sj2 = seD0 + 2 * j + 2

            tbj = tb_bf if j % 2 == 0 else tb2_bf

            @block.vector
            def _(v, j=j, s2=_sj2, dxr=_dj_xr, tbj=tbj):
                v.wait_ge(se.h, s2)
                v.wait_ge(dS.h, dxr)
                if j >= 2:
                    v.wait_ge(dTi[j % 2].h, d_tint[j - 2])
                v.reciprocal(istd[:], veps[:])
                v.drain()
                v.tensor_scalar(scrB[:], scrA[:], mv[:, 0:1], istd[:, 0:1],
                                ALU.subtract, ALU.mult)
                v.drain()
                v.reduce_max(amax[:, 0:1], scrB[:], axis=AX.X,
                             apply_absolute_value=True)
                v.drain()
                v.tensor_scalar_max(acl[:, 0:1], amax[:, 0:1], 1e-5)
                v.drain()
                v.reciprocal(t1[:, 0:1], acl[:, 0:1])
                v.drain()
                v.tensor_scalar_mul(a_sb[:, j:j + 1], t1[:, 0:1], 127.0)
                v.drain()
                v.tensor_scalar(scg_sb[:, j:j + 1], acl[:, 0:1], mwg_b[:, 0:1],
                                1.0 / 127.0, ALU.mult, ALU.mult)
                v.drain()
                v.tensor_scalar(scu_sb[:, j:j + 1], acl[:, 0:1], mwu_b[:, 0:1],
                                1.0 / 127.0, ALU.mult, ALU.mult)
                v.drain()
                v.tensor_scalar_mul(xcs[:], scrB[:], a_sb[:, j:j + 1])
                v.drain()
                v.tensor_copy(iscr[:], xcs[:])
                v.drain()
                v.tensor_copy(tbj[:], iscr[:])
                inc(ve, v.drain())
            _vj3 = ve.n

            @block.sync
            def _(sy, j=j, v3=_vj3, tbj=tbj):
                sy.wait_ge(ve.h, v3)
                inc(dTi[j % 2], sy.dma_start(
                    out=tint_d.ap()[j * 128:(j + 1) * 128, :], in_=tbj[:]), 16)
            d_tint[j] = dTi[j % 2].n
            d_done[j] = _vj3
        ve_D = d_done[NT - 1]

        @block.scalar
        def _(s):
            s.wait_ge(ve.h, vS3)
            s.activation(ef16[:], m_sb[:], AF.Exp, bias=nbg[:, 0:1], scale=SCL)
            inc(se, s.drain())
            s.wait_ge(ve.h, vS6)
            s.activation(e8[:, :], m8[:, :], AF.Exp, bias=g2[:, 0:1], scale=SCL)
            inc(se, s.drain())
            s.wait_ge(ve.h, vS8)
            s.activation(ef16[:], m_sb[:], AF.Exp, bias=nbg[:, 0:1], scale=SCL)
            inc(se, s.drain())
            for j in range(NT):
                s.wait_ge(ve.h, d_v1[j])
                s.activation(veps[:], mv[:, 1:2], AF.Sqrt, bias=eps_sb[:, 0:1])
                inc(se, s.drain())
                s.wait_ge(ve.h, d_v2[j])
                s.activation(veps[:], mv[:, 1:2], AF.Sqrt, bias=eps_sb[:, 0:1])
                inc(se, s.drain())
        assert se.n == seC_end + 3 + 2 * NT

        @block.scalar
        def _(s):
            s.wait_ge(ve.h, ve_D)
            s.wait_ge(dA2[0].h, dA_wT[0])
            s.wait_ge(dA2[1].h, dA_wT[1])
            for ic in range(ICN):
                inc(dWd, s.dma_start(
                    out=wdT_sb[:, ic * D:(ic + 1) * D],
                    in_=wdT_d.ap()[ic * 128:(ic + 1) * 128, :]), 16)
        dWd_mark = dWd.n

        e1_te_slice = {}
        e1_ve_h = {}
        e1_te_g = {}
        e2_v1 = {}
        e2_vo = {}
        so_marks = {}
        seE0 = seC_end + 3 + 2 * NT
        bounce_prev = [d_tint[NT - 2], d_tint[NT - 1]]
        hT_prev = [dKV[0].n, dKV[1].n]
        bseq = [0]
        tT_mark = {}

        def emit_tintT(ch):
            rows0 = ch * CHT

            @block.sync
            def _(sy, ch=ch, rows0=rows0):
                sy.wait_ge(dTi[0].h, d_tint[4 * ch + 2])
                sy.wait_ge(dTi[1].h, d_tint[4 * ch + 3])
                if ch >= 1:
                    sy.wait_ge(te.h, e1_te_slice[ch * NIN - 1])
                for dc in range(DC):
                    inc(dTT, sy.dma_start_transpose(
                        out=tintT[:, dc * CHT:(dc + 1) * CHT],
                        in_=tint_d.ap()[rows0:rows0 + CHT,
                                        dc * 128:(dc + 1) * 128]), 16)
            tT_mark[ch] = dTT.n

        emit_tintT(0)
        for ch in range(NCH):
            rows0 = ch * CHT
            hq_mark = {}
            hq_last = {}
            se0 = seE0 + 36 * ch
            _dS_tT = tT_mark[ch]

            for n in range(NIN):
                sidx = ch * NIN + n
                par = n % 2

                @block.sync
                def _(sy, n=n, par=par, sidx=sidx):
                    if sidx >= 2:
                        sy.wait_ge(te.h, e1_te_slice[sidx - 2])
                    else:
                        sy.wait_ge(te.h, teC_end)
                        sy.wait_ge(dA2[0].h, dA_wT[0])
                        sy.wait_ge(dA2[1].h, dA_wT[1])
                    for dc in range(DC):
                        inc(dWn[par], sy.dma_start(
                            out=wgn[par][:, dc * 512:(dc + 1) * 512],
                            in_=wgT_d.ap()[dc * 128:(dc + 1) * 128,
                                           n * 512:(n + 1) * 512]), 16)

                @block.gpsimd
                def _(g, n=n, par=par, sidx=sidx):
                    if sidx >= 2:
                        g.wait_ge(te.h, e1_te_slice[sidx - 2])
                    else:
                        g.wait_ge(te.h, teC_end)
                        g.wait_ge(dA2[0].h, dA_wT[0])
                        g.wait_ge(dA2[1].h, dA_wT[1])
                    for dc in range(DC):
                        inc(dW8[par], g.dma_start(
                            out=wun[par][:, dc * 512:(dc + 1) * 512],
                            in_=wuT_d.ap()[dc * 128:(dc + 1) * 128,
                                           n * 512:(n + 1) * 512]), 16)
                _dS_wn = (par, dWn[par].n, dW8[par].n)

                for tl in range(CTL):
                    gidx = ch * CTL + tl
                    gk = sidx * CTL + tl
                    gps = ps_b0 if gk % 2 == 0 else ps_b1

                    @block.tensor
                    def _(t, tl=tl, par=par, gps=gps, gk=gk, ch=ch,
                          dwn=_dS_wn, dtT=_dS_tT):
                        t.wait_ge(dTT.h, dtT)
                        t.wait_ge(dWn[dwn[0]].h, dwn[1])
                        t.wait_ge(dW8[dwn[0]].h, dwn[2])
                        if gk >= 2:
                            t.wait_ge(ve.h, e1_ve_h[gk - 2])
                        if ch > 0 and gk < ch * NIN * CTL + 2:
                            t.wait_ge(ve.h, e2_v1[4 * (ch - 1) + 2 + (gk % 2)])
                        for dc in range(DC):
                            lhs = tintT[:, dc * CHT + tl * 128: dc * CHT + (tl + 1) * 128]
                            t.matmul(gps[:, 0:512], lhs,
                                     wgn[par][:, dc * 512:(dc + 1) * 512],
                                     start=(dc == 0), stop=(dc == DC - 1))
                            mm = t.matmul(gps[:, 512:1024], lhs,
                                          wun[par][:, dc * 512:(dc + 1) * 512],
                                          start=(dc == 0), stop=(dc == DC - 1))
                        inc(te, mm)
                    _te_g = te.n
                    e1_te_slice[sidx] = te.n
                    gkl = n * CTL + tl
                    e1_te_g[(ch, gkl)] = te.n

                    gb = g_bf[:, (gk % 2) * 512:(gk % 2) * 512 + 512]
                    _se_g = se0 + gkl + 1

                    @block.vector
                    def _(v, n=n, tl=tl, gidx=gidx, gps=gps, tg=_te_g,
                          sg=_se_g, gb=gb):
                        v.wait_ge(te.h, tg)
                        v.wait_ge(se.h, sg)
                        v.scalar_tensor_tensor(
                            h_sb[:, tl * INNER + n * 512: tl * INNER + (n + 1) * 512],
                            gps[:, 512:1024], scu_sb[:, gidx:gidx + 1], gb,
                            ALU.mult, ALU.mult)
                        inc(ve, v.drain())
                    e1_ve_h[gk] = ve.n

                    if n == NIN - 1:
                        @block.vector
                        def _(v, tl=tl, gidx=gidx):
                            v.reduce_max(amax[:, 0:1],
                                         h_sb[:, tl * INNER:(tl + 1) * INNER],
                                         axis=AX.X, apply_absolute_value=True)
                            v.drain()
                            v.tensor_scalar_max(acl[:, 0:1],
                                                amax[:, 0:1], 1e-5)
                            v.drain()
                            v.reciprocal(t1[:, 0:1], acl[:, 0:1])
                            v.drain()
                            v.tensor_scalar_mul(ah_s[:, 0:1], t1[:, 0:1], 127.0)
                            v.drain()
                            v.tensor_scalar(cd_sb[:, gidx:gidx + 1], acl[:, 0:1],
                                            mwd_b[:, 0:1], 1.0 / 127.0,
                                            ALU.mult, ALU.mult)
                            inc(ve, v.drain())

                        for sl in range(4):
                            bpar = bseq[0] % 2
                            bseq[0] += 1
                            tbq = tb_bf if bpar == 0 else tb2_bf
                            prev = bounce_prev[bpar]

                            @block.vector
                            def _(v, tl=tl, sl=sl, tbq=tbq, prev=prev,
                                  bpar=bpar):
                                v.wait_ge(dTi[bpar].h, prev)
                                v.tensor_scalar_mul(
                                    xcs[:],
                                    h_sb[:, tl * INNER + sl * 1024: tl * INNER + (sl + 1) * 1024],
                                    ah_s[:, 0:1])
                                v.drain()
                                v.tensor_copy(iscr[:], xcs[:])
                                v.drain()
                                v.tensor_copy(tbq[:], iscr[:])
                                inc(ve, v.drain())
                            _qv = ve.n

                            @block.sync
                            def _(sy, ch=ch, tl=tl, sl=sl, tbq=tbq, qv=_qv,
                                  bpar=bpar):
                                sy.wait_ge(ve.h, qv)
                                inc(dTi[bpar], sy.dma_start(
                                    out=hint_d.ap()[ch * CHT + tl * 128: ch * CHT + (tl + 1) * 128,
                                                    sl * 1024:(sl + 1) * 1024],
                                    in_=tbq[:]), 16)
                            bounce_prev[bpar] = dTi[bpar].n
                            hq_mark[(tl, sl)] = (bpar, dTi[bpar].n)
                            hq_last[(sl, bpar)] = dTi[bpar].n

            if ch + 1 < NCH:
                emit_tintT(ch + 1)

            hTm = []
            for ic in range(ICN):
                hp = ic % 2

                @block.sync
                def _(sy, ic=ic, rows0=rows0, hq=dict(hq_last), hp=hp,
                      hprev=hT_prev[hp]):
                    for qp in (0, 1):
                        if (ic // 8, qp) in hq:
                            sy.wait_ge(dTi[qp].h, hq[(ic // 8, qp)])
                    if hprev:
                        sy.wait_ge(dKV[hp].h, hprev)
                    inc(dKV[hp], sy.dma_start_transpose(
                        out=hT_sb[:, ic * CHT:(ic + 1) * CHT],
                        in_=hint_d.ap()[rows0:rows0 + CHT,
                                        ic * 128:(ic + 1) * 128]), 16)
                hT_prev[hp] = dKV[hp].n
                hTm.append((hp, dKV[hp].n))

            for tl in range(CTL):
                gidx = ch * CTL + tl
                psd = ps_b0 if tl % 2 == 0 else ps_b1

                @block.sync
                def _(sy, gidx=gidx):
                    if gidx >= 2:
                        sy.wait_ge(ve.h, e2_vo[gidx - 2])
                    else:
                        sy.wait_ge(ve.h, d_v2[NT - 2 + gidx])
                    xb = xb0 if gidx % 2 == 0 else xb1
                    inc(dX[gidx % 2], sy.dma_start(
                        out=xb[:],
                        in_=xres_d.ap()[gidx * 128:(gidx + 1) * 128, :]), 16)
                _dS_xr = (gidx % 2, dX[gidx % 2].n)

                @block.tensor
                def _(t, tl=tl, gidx=gidx, psd=psd, hTm=tuple(hTm),
                      dwd=dWd_mark):
                    t.wait_ge(dWd.h, dwd)
                    if tl >= 2:
                        t.wait_ge(ve.h, e2_v1[gidx - 2])
                    if tl > 0:
                        t.wait_ge(dKV[hTm[-2][0]].h, hTm[-2][1])
                        t.wait_ge(dKV[hTm[-1][0]].h, hTm[-1][1])
                    for ic in range(ICN):
                        if tl == 0:
                            t.wait_ge(dKV[hTm[ic][0]].h, hTm[ic][1])
                        lhs = hT_sb[:, ic * CHT + tl * 128: ic * CHT + (tl + 1) * 128]
                        t.matmul(psd[:, 0:512], lhs,
                                 wdT_sb[:, ic * D: ic * D + 512],
                                 start=(ic == 0), stop=(ic == ICN - 1))
                        mm = t.matmul(psd[:, 512:1024], lhs,
                                      wdT_sb[:, ic * D + 512:(ic + 1) * D],
                                      start=(ic == 0), stop=(ic == ICN - 1))
                    inc(te, mm)
                _e2_te = te.n

                @block.vector
                def _(v, gidx=gidx, psd=psd, te_m=_e2_te):
                    v.wait_ge(te.h, te_m)
                    v.tensor_scalar_mul(scrA[:], psd[:], cd_sb[:, gidx:gidx + 1])
                    v.drain()
                    v.bn_stats(st12[:, 0:6], scrA[:, 0:512])
                    v.bn_stats(st12[:, 6:12], scrA[:, 512:1024])
                    v.drain()
                    v.bn_aggr(mv[:], st12[:])
                    inc(ve, v.drain())
                e2_v1[gidx] = ve.n

                _e2_s1 = se0 + 32 + tl + 1

                @block.vector
                def _(v, gidx=gidx, s1=_e2_s1, dxr=_dS_xr):
                    v.wait_ge(se.h, s1)
                    v.wait_ge(dX[dxr[0]].h, dxr[1])
                    if gidx >= 1:
                        v.wait_ge(so.h, so_marks[gidx - 1])
                    xb = xb0 if gidx % 2 == 0 else xb1
                    v.reciprocal(istd[:], veps[:])
                    v.drain()
                    v.tensor_scalar(scrB[:], scrA[:], mv[:, 0:1], istd[:, 0:1],
                                    ALU.subtract, ALU.mult)
                    v.drain()
                    v.tensor_add(scrB[:], scrB[:], xb[:])
                    inc(ve, v.drain())
                e2_vo[gidx] = ve.n

                @block.sync
                def _(sy, gidx=gidx, vo=e2_vo[gidx]):
                    sy.wait_ge(ve.h, vo)
                    inc(so, sy.dma_start(
                        out=out_d.ap()[gidx * 128:(gidx + 1) * 128, :],
                        in_=scrB[:]), 16)
                so_marks[gidx] = so.n

            @block.scalar
            def _(s, ch=ch):
                for gkl in range(NIN * CTL):
                    gk = ch * NIN * CTL + gkl
                    tl = gkl % CTL
                    gidx = ch * CTL + tl
                    gb = g_bf[:, (gk % 2) * 512:(gk % 2) * 512 + 512]
                    gps = ps_b0 if gk % 2 == 0 else ps_b1
                    s.wait_ge(te.h, e1_te_g[(ch, gkl)])
                    if gk >= 2:
                        s.wait_ge(ve.h, e1_ve_h[gk - 2])
                    s.activation(gb, gps[:, 0:512], AF.Silu,
                                 scale=scg_sb[:, gidx:gidx + 1])
                    inc(se, s.drain())
                for tl in range(CTL):
                    gidx = ch * CTL + tl
                    s.wait_ge(ve.h, e2_v1[gidx])
                    s.activation(veps[:], mv[:, 1:2], AF.Sqrt,
                                 bias=eps_sb[:, 0:1])
                    inc(se, s.drain())
            assert se.n == seE0 + 36 * (ch + 1), (se.n, seE0, ch)

        @block.sync
        def _(sy):
            sy.wait_ge(so.h, so.n)

    return nc


_nc_cache = None


def _get_nc():
    global _nc_cache
    if _nc_cache is None:
        _nc_cache = build()
    return _nc_cache


def kernel(**inputs):
    x = np.asarray(inputs["L0_attn_x"], dtype=np.float32).reshape(T, D)
    wq = np.ascontiguousarray(np.asarray(inputs["L0_attn_Wq"], np.float32))
    wk = np.ascontiguousarray(np.asarray(inputs["L0_attn_Wk"], np.float32))
    wv = np.ascontiguousarray(np.asarray(inputs["L0_attn_Wv"], np.float32))
    bq = np.ascontiguousarray(np.asarray(inputs["L0_attn_bq"], np.float32))
    bk = np.ascontiguousarray(np.asarray(inputs["L0_attn_bk"], np.float32))
    bv = np.ascontiguousarray(np.asarray(inputs["L0_attn_bv"], np.float32))
    wg = np.ascontiguousarray(np.asarray(inputs["L0_ffn_w_gate"], np.float32))
    wu = np.ascontiguousarray(np.asarray(inputs["L0_ffn_w_up"], np.float32))
    wd = np.ascontiguousarray(np.asarray(inputs["L0_ffn_w_down"], np.float32))

    xt = x.reshape(T // 128, 128, D)
    in_maps = []
    for c in range(NCORES):
        idx = np.arange(NT) * NCORES + c
        x_own = np.ascontiguousarray(xt[idx].reshape(TPC, D))
        x_ctg = np.ascontiguousarray(x[c * TPC:(c + 1) * TPC])
        in_maps.append({
            "x_own": x_own, "x_ctg": x_ctg,
            "wq": wq, "wk": wk, "wv": wv,
            "bq": bq, "bk": bk, "bv": bv,
            "wg": wg, "wu": wu, "wd": wd,
            "cid": np.array([[c]], np.float32),
        })

    nc = _get_nc()
    res = run_bass_kernel_spmd(nc, in_maps, core_ids=list(range(NCORES)))

    out = np.empty((T // 128, 128, D), np.float32)
    for c in range(NCORES):
        idx = np.arange(NT) * NCORES + c
        out[idx] = np.asarray(res.results[c]["out"], np.float32).reshape(NT, 128, D)
    return out.reshape(1, T, D)


# version 19
# speedup vs baseline: 1.1201x; 1.1201x over previous
import numpy as np
import ml_dtypes
from contextlib import ExitStack

import concourse.bass as bass
import concourse.mybir as mybir
from concourse.bass_utils import run_bass_kernel_spmd

F32 = mybir.dt.float32
BF16 = mybir.dt.bfloat16
I8 = mybir.dt.int8
I32 = mybir.dt.int32
AF = mybir.ActivationFunctionType
ALU = mybir.AluOpType
AX = mybir.AxisListType
BF = ml_dtypes.bfloat16

NCORES = 8
T, D, INNER = 16384, 1024, 4096
TPC = T // NCORES
NT = TPC // 128
KB = 1024
NB = T // KB
DC = 8
ICN = 32
NIN = 8
NCH = 4
CTL = NT // NCH
CHT = 128 * CTL
SCL = 1.0 / 32.0
EPS = 1e-5
NEG = -1e9
NWT = 32
N_IT = NB * (NB + 1) // 2


def build():
    nc = bass.Bass("TRN2", target_bir_lowering=False)

    x_own_d = nc.dram_tensor("x_own", [TPC, D], F32, kind="ExternalInput")
    x_ctg_d = nc.dram_tensor("x_ctg", [TPC, D], F32, kind="ExternalInput")
    wq_d = nc.dram_tensor("wq", [D, D], F32, kind="ExternalInput")
    wk_d = nc.dram_tensor("wk", [D, D], F32, kind="ExternalInput")
    wv_d = nc.dram_tensor("wv", [D, D], F32, kind="ExternalInput")
    bq_d = nc.dram_tensor("bq", [D], F32, kind="ExternalInput")
    bk_d = nc.dram_tensor("bk", [D], F32, kind="ExternalInput")
    bv_d = nc.dram_tensor("bv", [D], F32, kind="ExternalInput")
    wg_d = nc.dram_tensor("wg", [INNER, D], F32, kind="ExternalInput")
    wu_d = nc.dram_tensor("wu", [INNER, D], F32, kind="ExternalInput")
    wd_d = nc.dram_tensor("wd", [D, INNER], F32, kind="ExternalInput")
    cid_d = nc.dram_tensor("cid", [1, 1], F32, kind="ExternalInput")
    out_d = nc.dram_tensor("out", [TPC, D], F32, kind="ExternalOutput")

    xn_own_d = nc.dram_tensor("xn_own_s", [TPC, D], BF16)
    xn_ctg_d = nc.dram_tensor("xn_ctg_s", [TPC, D], BF16)
    KVSZ = D * TPC
    ccKVi = nc.dram_tensor("ccKVi", [2 * KVSZ], BF16)
    ccKVo = nc.dram_tensor("ccKVo", [NCORES, 2 * KVSZ], BF16, addr_space="Shared")
    sti = nc.dram_tensor("sti", [1, 16], F32)
    sto_d = nc.dram_tensor("sto", [NCORES, 16], F32, addr_space="Shared")
    wgt_d = nc.dram_tensor("wgt_s", [INNER, D], BF16)
    wut_d = nc.dram_tensor("wut_s", [INNER, D], BF16)
    wdt_d = nc.dram_tensor("wdt_s", [D, INNER], BF16)
    wgT_d = nc.dram_tensor("wgT_s", [D, INNER], BF16)
    wuT_d = nc.dram_tensor("wuT_s", [D, INNER], BF16)
    wgT8_d = nc.dram_tensor("wgT8_s", [D, INNER], I8)
    wuT8_d = nc.dram_tensor("wuT8_s", [D, INNER], I8)
    wdT_d = nc.dram_tensor("wdT_s", [INNER, D], BF16)
    tint_d = nc.dram_tensor("tint_s", [TPC, D], BF16)
    hint_d = nc.dram_tensor("hint_s", [TPC, INNER], BF16)
    xres_d = nc.dram_tensor("xres_s", [TPC, D], F32)
    col_d = nc.dram_tensor("col_s", [128], F32)
    sc2_d = nc.dram_tensor("sc2_s", [16], F32)
    wcol3_d = nc.dram_tensor("wcol3_s", [384], F32)
    wsc12_d = nc.dram_tensor("wsc12_s", [12], F32)

    ident_dr = nc.inline_tensor(np.eye(128, dtype=BF), "ident_i")

    with ExitStack() as ctx:
        e = ctx.enter_context

        ident = e(nc.sbuf_tensor("ident", [128, 128], BF16))
        mask = e(nc.sbuf_tensor("mask_sb", [128, KB], F32))
        bq_sb = e(nc.sbuf_tensor("bq_sb", [128, 8], F32))
        bk_sb = e(nc.sbuf_tensor("bk_sb", [128, 8], F32))
        eps_sb = e(nc.sbuf_tensor("eps_sb", [128, 1], F32))
        cid_b = e(nc.sbuf_tensor("cid_b", [128, 1], F32))
        thr = e(nc.sbuf_tensor("thr", [128, 1], F32))
        m_sb = e(nc.sbuf_tensor("m_sb", [128, NT], F32))
        z_sb = e(nc.sbuf_tensor("z_sb", [128, NT], F32))
        f_sb = e(nc.sbuf_tensor("f_sb", [128, NT], F32))
        a_sb = e(nc.sbuf_tensor("a_sb", [128, NT], F32))
        cd_sb = e(nc.sbuf_tensor("cd_sb", [128, NT], F32))
        scg_sb = e(nc.sbuf_tensor("scg_sb", [128, NT], F32))
        scu_sb = e(nc.sbuf_tensor("scu_sb", [128, NT], F32))
        ef16 = e(nc.sbuf_tensor("ef16", [128, NT], F32))
        rowmax = e(nc.sbuf_tensor("rowmax", [128, 1], F32))
        d2 = e(nc.sbuf_tensor("d2", [128, 2], F32))
        cfac_a = e(nc.sbuf_tensor("cfac_a", [128, 2], F32))
        cfac = e(nc.sbuf_tensor("cfac", [128, 2], F32))
        nbias = e(nc.sbuf_tensor("nbias", [128, 2], F32))
        zblk = e(nc.sbuf_tensor("zblk", [128, 2], F32))
        t1 = e(nc.sbuf_tensor("t1", [128, 1], F32))
        mv = e(nc.sbuf_tensor("mv", [128, 2], F32))
        st12 = e(nc.sbuf_tensor("st12", [128, 12], F32))
        veps = e(nc.sbuf_tensor("veps", [128, 1], F32))
        istd = e(nc.sbuf_tensor("istd", [128, 1], F32))
        amax = e(nc.sbuf_tensor("amax", [128, 1], F32))
        hsl = e(nc.sbuf_tensor("hsl", [128, 1], F32))
        ham16 = e(nc.sbuf_tensor("ham16", [128, 16], F32))
        acl = e(nc.sbuf_tensor("acl", [128, 1], F32))
        ah_s = e(nc.sbuf_tensor("ah_s", [128, 1], F32))
        MgB = e(nc.sbuf_tensor("MgB", [128, 1], F32))
        ZgB = e(nc.sbuf_tensor("ZgB", [128, 1], F32))
        rz = e(nc.sbuf_tensor("rz", [128, 1], F32))
        nbg = e(nc.sbuf_tensor("nbg", [128, 1], F32))
        mwg_b = e(nc.sbuf_tensor("mwg_b", [128, 1], F32))
        mwu_b = e(nc.sbuf_tensor("mwu_b", [128, 1], F32))
        mwd_b = e(nc.sbuf_tensor("mwd_b", [128, 1], F32))
        wthr6 = e(nc.sbuf_tensor("wthr6", [128, 6], F32))
        wacc3 = e(nc.sbuf_tensor("wacc3", [128, 3], F32))
        wred = e(nc.sbuf_tensor("wred", [128, 1], F32))
        w3r = e(nc.sbuf_tensor("w3r", [3, 128], F32))
        wsum3 = e(nc.sbuf_tensor("wsum3", [3, 1], F32))
        wquad = e(nc.sbuf_tensor("wquad", [3, 4], F32))
        strow = e(nc.sbuf_tensor("strow", [1, 128], F32))
        strow2 = e(nc.sbuf_tensor("strow2", [1, 128], F32))
        stg2 = e(nc.sbuf_tensor("stg2", [1, 16], F32))
        m8 = e(nc.sbuf_tensor("m8", [1, 8], F32))
        z8 = e(nc.sbuf_tensor("z8", [1, 8], F32))
        e8 = e(nc.sbuf_tensor("e8", [1, 8], F32))
        g1 = e(nc.sbuf_tensor("g1", [1, 1], F32))
        g2 = e(nc.sbuf_tensor("g2", [1, 1], F32))
        p_sb = e(nc.sbuf_tensor("p_sb", [128, 2 * KB], BF16))
        pT_sb = e(nc.sbuf_tensor("pT_sb", [128, 2 * KB], BF16))
        xb0 = e(nc.sbuf_tensor("xb0", [128, D], F32))
        xb1 = e(nc.sbuf_tensor("xb1", [128, D], F32))
        wstr0 = e(nc.sbuf_tensor("wstr0", [128, 1024], F32))
        wstr1 = e(nc.sbuf_tensor("wstr1", [128, 1024], F32))

        wneg = e(nc.sbuf_tensor("wneg", [128, 1024], F32))
        wtrn = e(nc.sbuf_tensor("wtrn", [128, 2048], BF16))
        wtmp = e(nc.sbuf_tensor("wtmp", [128, 2048], BF16))

        AR = 160 * 1024
        got = nc.bump_sbuf(AR, side="right")
        assert got is not None, "arena alloc failed"
        ar0 = got[0]

        def at(name, shape, dtype, off):
            return nc.alloc_sbuf_tensor_at(name, shape, dtype, offset=ar0 + off)

        QT = at("QT", [128, 8 * TPC], BF16, 0)
        xnTo = at("xnTo", [128, 8 * TPC], BF16, 32 * 1024)
        xnTc = at("xnTc", [128, 8 * TPC], BF16, 64 * 1024)
        w_cur = at("w_cur", [128, 8 * D], BF16, 96 * 1024)
        N_sb = at("N_sb", [128, NT * D], F32, 32 * 1024)
        kblk2 = [at("kblk0", [128, 8 * KB], BF16, 96 * 1024),
                 at("kblk1", [128, 8 * KB], BF16, 112 * 1024)]
        vblk2 = [at("vblk0", [128, 8 * KB], BF16, 128 * 1024),
                 at("vblk1", [128, 8 * KB], BF16, 144 * 1024)]
        tintT = at("tintT", [128, 8 * CHT], BF16, 0)
        xcs = at("xcs", [128, D], F32, 8 * 1024)
        iscr = at("iscr", [128, D], I32, 12 * 1024)
        tb_bf = at("tb_bf", [128, D], BF16, 16 * 1024)
        tb2_bf = at("tb2_bf", [128, D], BF16, 18 * 1024)
        scrA = at("scrA", [128, D], F32, 20 * 1024)
        scrB = at("scrB", [128, D], F32, 24 * 1024)
        g_bf = at("g_bf", [128, 1024], BF16, 28 * 1024)
        u_bf = at("u_bf", [128, 1024], BF16, 30 * 1024)
        bv_bc = at("bv_bc", [128, D], F32, 154 * 1024)
        kvstg0 = at("kvstg0", [128, 512], BF16, 158 * 1024)
        kvstg1 = at("kvstg1", [128, 512], BF16, 159 * 1024)
        wstr2 = at("wstr2", [128, 1024], F32, 112 * 1024)
        wstr3 = at("wstr3", [128, 1024], F32, 116 * 1024)

        wdT_sb = at("wdT_sb", [128, ICN * D], BF16, 32 * 1024)
        h_sb = at("h_sb", [128, CTL * INNER], BF16, 96 * 1024)
        hT_sb = at("hT_sb", [128, ICN * CHT], BF16, 96 * 1024)
        wgn = [at("wgn0", [128, 8 * 512], BF16, 128 * 1024),
               at("wgn1", [128, 8 * 512], BF16, 144 * 1024)]
        wun = [at("wun0", [128, 8 * 512], BF16, 136 * 1024),
               at("wun1", [128, 8 * 512], BF16, 152 * 1024)]

        ps_b0 = e(nc.psum_tensor("ps_b0", [128, 1024], F32))
        ps_b1 = e(nc.psum_tensor("ps_b1", [128, 1024], F32))
        ps_av = e(nc.psum_tensor("ps_av", [128, 1024], F32))
        ps_c = e(nc.psum_tensor("ps_c", [128, 2048], BF16))

        class Sem:
            def __init__(self, name):
                self.h = e(nc.semaphore(name))
                self.n = 0

        dS = Sem("dS")
        dA2 = [Sem("dA0"), Sem("dA1")]
        dG = Sem("dG")
        dW = Sem("dW")
        dKV = [Sem("dKV0"), Sem("dKV1")]
        dKe = [Sem("dKe0"), Sem("dKe1")]
        dTe = [Sem("dTe0"), Sem("dTe1")]
        dGa = [Sem("dGa0"), Sem("dGa1")]
        dGa2 = Sem("dGa2")
        dGa3 = Sem("dGa3")
        dXa = [Sem("dXa0"), Sem("dXa1")]
        dGb = [Sem("dGb0"), Sem("dGb1")]
        dX = [Sem("dX0"), Sem("dX1")]
        dWn = [Sem("dWn0"), Sem("dWn1")]
        dWd = Sem("dWd")
        dTi = [Sem("dTi0"), Sem("dTi1")]
        dTT = Sem("dTT")
        dPw = [Sem("dPw0"), Sem("dPw1")]
        dW8 = [Sem("dW80"), Sem("dW81")]
        dHT = [Sem("dHT0"), Sem("dHT1")]
        cc = Sem("cc")
        te = Sem("te")
        ve = Sem("ve")
        se = Sem("se")
        sw = Sem("sw")
        vw = Sem("vw")
        ge = Sem("ge")
        so = Sem("so")

        def inc(sem, instr, k=1):
            instr.then_inc(sem.h, k)
            sem.n += k

        block = e(nc.Block())

        @block.sync
        def _(sy):
            for dst, src in [
                (ident[:], ident_dr.ap()),
                (bq_sb[:], bass.AP(bq_d, 0, [[1, 128], [128, 8]])),
                (bk_sb[:], bass.AP(bk_d, 0, [[1, 128], [128, 8]])),
                (bv_bc[:], bass.AP(bv_d, 0, [[0, 128], [1, D]])),
                (cid_b[:], bass.AP(cid_d, 0, [[0, 128], [1, 1]])),
            ]:
                inc(dS, sy.dma_start(out=dst, in_=src,
                                     allow_slow_non_contiguous=True), 16)
        dS_setup = dS.n

        @block.vector
        def _(v):
            v.memset(eps_sb[:], EPS)
            v.drain()
            v.memset(stg2[:, :], 0.0)
            inc(ve, v.drain())

        @block.gpsimd
        def _(g):
            g.wait_ge(dS.h, dS_setup)
            g.iota(iscr[:], [[1, KB]], base=0, channel_multiplier=-1)
            inc(ge, g.drain())
        ge_iota = ge.n

        @block.vector
        def _(v):
            v.wait_ge(ge.h, ge_iota)
            v.tensor_scalar_mul(thr[:], cid_b[:], 128.0)
            v.drain()
            v.tensor_copy(mask[:], iscr[:])
            v.drain()
            v.tensor_single_scalar(mask[:], mask[:], thr[:, 0:1], ALU.is_le)
            v.drain()
            v.tensor_scalar(mask[:], mask[:], 1.0, -NEG, ALU.subtract, ALU.mult)
            inc(ve, v.drain())

        wins = [wg_d, wu_d, wd_d]
        wterns = [wgt_d, wut_d, wdt_d]

        def wtile_ap(wi, i):
            w = wins[wi]
            colchunks = w.shape[1] // 1024
            r, cch = divmod(i, colchunks)
            return w.ap()[r * 128:(r + 1) * 128, cch * 1024:(cch + 1) * 1024]

        def wtern_ap(wi, i):
            w = wterns[wi]
            colchunks = w.shape[1] // 1024
            r, cch = divmod(i, colchunks)
            return w.ap()[r * 128:(r + 1) * 128, cch * 1024:(cch + 1) * 1024]

        ln_srcs = [x_own_d, x_ctg_d]
        ln_dsts = [xn_own_d, xn_ctg_d]
        dS_a1_base = dS.n
        ve_a1_base = ve.n
        se_a1_base = se.n

        def ve_stats(k):
            return ve_a1_base + 2 * k + 1

        def ve_xn(k):
            return ve_a1_base + 2 * k + 2

        def se_sqrt(k):
            return se_a1_base + k + 1


        @block.sync
        def _(sy):
            sy.wait_ge(dS.h, dS_setup)

            def ld(k):
                si, tt = divmod(k, NT)
                xb = xb0 if (k % 2 == 0) else xb1
                inc(dXa[k % 2], sy.dma_start(
                    out=xb[:],
                    in_=ln_srcs[si].ap()[tt * 128:(tt + 1) * 128, :]), 16)
                assert dXa[k % 2].n == 16 * (k // 2 + 1)

            ld(0)
            ld(1)
            for k in range(32):
                sy.wait_ge(ve.h, ve_xn(k))
                if k + 2 < 32:
                    ld(k + 2)
                si, tt = divmod(k, NT)
                tbk = tb_bf if k % 2 == 0 else tb2_bf
                inc(dTi[k % 2], sy.dma_start(
                    out=ln_dsts[si].ap()[tt * 128:(tt + 1) * 128, :],
                    in_=tbk[:]), 16)
                assert dTi[k % 2].n == 16 * (k // 2 + 1)
        a1_st_done = (dTi[0].n, dTi[1].n)
        dS_a1_done = dS.n

        @block.vector
        def _(v):
            wbufs = [wstr0, wstr1, wstr2, wstr3]
            wsems = [dGa[0], dGa[1], dGa2, dGa3]
            v.memset(wacc3[:], 0.0)
            v.drain()
            for k in range(32):
                xb = xb0 if (k % 2 == 0) else xb1
                tbk = tb_bf if k % 2 == 0 else tb2_bf
                v.wait_ge(dXa[k % 2].h, 16 * (k // 2 + 1))
                v.bn_stats(st12[:, 0:6], xb[:, 0:512])
                v.bn_stats(st12[:, 6:12], xb[:, 512:1024])
                v.drain()
                v.bn_aggr(mv[:], st12[:])
                inc(ve, v.drain())
                v.wait_ge(se.h, se_sqrt(k))
                if k >= 2:
                    v.wait_ge(dTi[k % 2].h, 16 * (k // 2))
                v.reciprocal(istd[:], veps[:])
                v.drain()
                v.tensor_scalar(xcs[:], xb[:], mv[:, 0:1], istd[:, 0:1],
                                ALU.subtract, ALU.mult)
                v.drain()
                v.tensor_copy(tbk[:], xcs[:])
                inc(ve, v.drain())
                for i3 in range(3):
                    i = 3 * k + i3
                    wi, wt = divmod(i, NWT)
                    v.wait_ge(wsems[i % 4].h, 16 * (i // 4 + 1))
                    v.reduce_sum(wred[:, 0:1], wbufs[i % 4][:], axis=AX.X,
                                 apply_absolute_value=True)
                    v.drain()
                    v.tensor_add(wacc3[:, wi:wi + 1], wred[:, 0:1],
                                 wacc3[:, wi:wi + 1])
                    inc(sw, v.drain())
        assert ve.n == ve_a1_base + 64
        assert sw.n == 96

        @block.scalar
        def _(s):
            wbufs = [wstr0, wstr1, wstr2, wstr3]
            wsems = [dGa[0], dGa[1], dGa2, dGa3]

            def ld_even(iq):
                if iq < 96 and iq % 2 == 0:
                    if iq >= 4:
                        s.wait_ge(sw.h, iq - 3)
                    inc(wsems[iq % 4], s.dma_start(
                        out=wbufs[iq % 4][:], in_=wtile_ap(*divmod(iq, NWT))),
                        16)

            ld_even(0)
            ld_even(2)
            for k in range(32):
                s.wait_ge(ve.h, ve_stats(k))
                s.activation(veps[:], mv[:, 1:2], AF.Sqrt, bias=eps_sb[:, 0:1])
                inc(se, s.drain())
                for iq in range(3 * k + 4, 3 * k + 7):
                    ld_even(iq)
        assert se.n == se_a1_base + 32
        assert dGa[0].n + dGa2.n == 16 * 48

        @block.sync
        def _(sy):
            sy.wait_ge(sw.h, 96)
            sy.wait_ge(dS.h, dS_a1_done)
            inc(dS, sy.dma_start(out=wcol3_d.ap()[:], in_=wacc3[:, :]), 16)
            sy.wait_ge(dS.h, dS.n)
            inc(dS, sy.dma_start(out=w3r[:, :],
                                 in_=bass.AP(wcol3_d, 0, [[1, 3], [3, 128]]),
                                 allow_slow_non_contiguous=True), 16)
        dS_w3r = dS.n

        @block.vector
        def _(v):
            v.wait_ge(dS.h, dS_w3r)
            v.memset(wquad[:, 3:4], 0.0)
            v.drain()
            v.reduce_sum(wsum3[:, 0:1], w3r[:, :], axis=AX.X)
            v.drain()
            v.tensor_scalar_mul(wsum3[:, 0:1], wsum3[:, 0:1], 1.0 / (INNER * D))
            v.drain()
            v.tensor_scalar_max(wquad[:, 0:1], wsum3[:, 0:1], 1e-5)
            v.drain()
            v.tensor_scalar_mul(wquad[:, 1:2], wquad[:, 0:1], 0.5)
            v.drain()
            v.tensor_scalar_mul(wquad[:, 2:3], wquad[:, 0:1], -0.5)
            inc(ve, v.drain())
        ve_wq = ve.n

        @block.sync
        def _(sy):
            sy.wait_ge(ve.h, ve_wq)
            inc(dS, sy.dma_start(out=wsc12_d.ap()[0:12], in_=wquad[:, :]), 16)
            sy.wait_ge(dS.h, dS.n)
            inc(dS, sy.dma_start(out=mwg_b[:],
                                 in_=bass.AP(wsc12_d, 0, [[0, 128], [1, 1]])), 16)
            inc(dS, sy.dma_start(out=mwu_b[:],
                                 in_=bass.AP(wsc12_d, 4, [[0, 128], [1, 1]])), 16)
            inc(dS, sy.dma_start(out=mwd_b[:],
                                 in_=bass.AP(wsc12_d, 8, [[0, 128], [1, 1]])), 16)
            for wi in range(3):
                inc(dS, sy.dma_start(
                    out=wthr6[:, 2 * wi:2 * wi + 2],
                    in_=bass.AP(wsc12_d, 4 * wi + 1, [[0, 128], [1, 2]])), 16)
        dS_means = dS.n

        @block.sync
        def _(sy):
            sy.wait_ge(dS.h, dS_means)
            sy.wait_ge(dTi[0].h, a1_st_done[0])
            sy.wait_ge(dTi[1].h, a1_st_done[1])
            for dc in range(DC):
                inc(dS, sy.dma_start_transpose(
                    out=xnTo[:, dc * TPC:(dc + 1) * TPC],
                    in_=xn_own_d.ap()[:, dc * 128:(dc + 1) * 128]), 16)
            for dc in range(DC):
                inc(dS, sy.dma_start_transpose(
                    out=xnTc[:, dc * TPC:(dc + 1) * TPC],
                    in_=xn_ctg_d.ap()[:, dc * 128:(dc + 1) * 128]), 16)
        dS_xnT = dS.n

        te_a2_base = te.n
        ve_a2_base = ve.n

        def te_qg(gi):
            return te_a2_base + gi + 1

        def te_kg(gi):
            return te_a2_base + 32 + gi + 1

        def te_vg(gi):
            return te_a2_base + 64 + gi + 1

        def ve_qe(gi):
            return ve_a2_base + gi + 1

        def ve_ke(gi):
            return ve_a2_base + 32 + gi + 1

        def ve_vee(gi):
            return ve_a2_base + 64 + gi + 1

        aps = [ps_b0[:, 0:512], ps_b0[:, 512:1024]]

        dW_wq = 16 * DC
        dW_wk = 32 * DC
        dW_wv = 48 * DC

        @block.tensor
        def _(t):
            t.wait_ge(dS.h, dS_xnT)
            t.wait_ge(dW.h, dW_wq)
            for gi in range(32):
                ec, cc4 = divmod(gi, 4)
                ps = aps[gi % 2]
                if gi >= 2:
                    t.wait_ge(ve.h, ve_qe(gi - 2))
                for dc in range(DC):
                    mm = t.matmul(ps,
                                  w_cur[:, dc * D + ec * 128: dc * D + (ec + 1) * 128],
                                  xnTo[:, dc * TPC + cc4 * 512: dc * TPC + (cc4 + 1) * 512],
                                  start=(dc == 0), stop=(dc == DC - 1))
                inc(te, mm)
            t.wait_ge(dW.h, dW_wk)
            for gi in range(32):
                ec, cc4 = divmod(gi, 4)
                ps = aps[gi % 2]
                t.wait_ge(ve.h, ve_ke(gi - 2))
                for dc in range(DC):
                    mm = t.matmul(ps,
                                  w_cur[:, dc * D + ec * 128: dc * D + (ec + 1) * 128],
                                  xnTc[:, dc * TPC + cc4 * 512: dc * TPC + (cc4 + 1) * 512],
                                  start=(dc == 0), stop=(dc == DC - 1))
                inc(te, mm)
            t.wait_ge(dW.h, dW_wv)
            for gi in range(32):
                tt2, cc2 = divmod(gi, 2)
                ps = aps[gi % 2]
                t.wait_ge(ve.h, ve_vee(gi - 2))
                for dc in range(DC):
                    mm = t.matmul(ps,
                                  xnTc[:, dc * TPC + tt2 * 128: dc * TPC + (tt2 + 1) * 128],
                                  w_cur[:, dc * D + cc2 * 512: dc * D + (cc2 + 1) * 512],
                                  start=(dc == 0), stop=(dc == DC - 1))
                inc(te, mm)
        assert te.n == te_a2_base + 96

        def dKe_kexp(gi):
            return 16 * (gi // 2 + 1)

        def dKe_vexp(gi):
            return 16 * (16 + gi // 2 + 1)

        def dTe_tern(i):
            return 16 * (i // 2 + 1)

        @block.vector
        def _(v):
            for gi in range(32):
                ec, cc4 = divmod(gi, 4)
                ps = aps[gi % 2]
                v.wait_ge(te.h, te_qg(gi))
                v.tensor_scalar_add(QT[:, ec * TPC + cc4 * 512: ec * TPC + (cc4 + 1) * 512],
                                    ps, bq_sb[:, ec:ec + 1])
                inc(ve, v.drain())
            for gi in range(32):
                ec, cc4 = divmod(gi, 4)
                ps = aps[gi % 2]
                stg = kvstg0 if gi % 2 == 0 else kvstg1
                v.wait_ge(te.h, te_kg(gi))
                if gi >= 2:
                    v.wait_ge(dKe[gi % 2].h, dKe_kexp(gi - 2))
                v.tensor_scalar_add(stg[:], ps, bk_sb[:, ec:ec + 1])
                inc(ve, v.drain())
            for gi in range(32):
                tt2, cc2 = divmod(gi, 2)
                ps = aps[gi % 2]
                stg = kvstg0 if gi % 2 == 0 else kvstg1
                v.wait_ge(te.h, te_vg(gi))
                if gi >= 2:
                    v.wait_ge(dKe[gi % 2].h, dKe_vexp(gi - 2))
                else:
                    v.wait_ge(dKe[gi % 2].h, 16 * 16)
                v.tensor_add(stg[:], ps, bv_bc[:, cc2 * 512:(cc2 + 1) * 512])
                inc(ve, v.drain())
        assert ve.n == ve_a2_base + 96

        @block.sync
        def _(sy):
            sy.wait_ge(dS.h, dS_xnT)
            for gi in range(32):
                ec, cc4 = divmod(gi, 4)
                stg = kvstg0 if gi % 2 == 0 else kvstg1
                sy.wait_ge(ve.h, ve_ke(gi))
                inc(dKe[gi % 2], sy.dma_start(
                    out=bass.AP(ccKVi, ec * 128 * TPC + cc4 * 512,
                                [[TPC, 128], [1, 512]]),
                    in_=stg[:]), 16)
            for gi in range(32):
                tt2, cc2 = divmod(gi, 2)
                stg = kvstg0 if gi % 2 == 0 else kvstg1
                sy.wait_ge(ve.h, ve_vee(gi))
                inc(dKe[gi % 2], sy.dma_start(
                    out=bass.AP(ccKVi, KVSZ + tt2 * 128 * D + cc2 * 512,
                                [[D, 128], [1, 512]]),
                    in_=stg[:]), 16)
            assert dKe[0].n == 16 * 32 and dKe[1].n == 16 * 32

        @block.gpsimd
        def _(g):
            for dc in range(DC):
                inc(dW, g.dma_start(out=w_cur[:, dc * D:(dc + 1) * D],
                                    in_=wq_d.ap()[dc * 128:(dc + 1) * 128, :]), 16)
            for i in range(1, 96, 2):
                wi, wt = divmod(i, NWT)
                buf = wstr1 if i % 4 == 1 else wstr3
                sem = dGa[1] if i % 4 == 1 else dGa3
                if i >= 4:
                    g.wait_ge(sw.h, i - 3)
                inc(sem, g.dma_start(out=buf[:], in_=wtile_ap(wi, wt)), 16)
            g.wait_ge(te.h, te_qg(31))
            for dc in range(DC):
                inc(dW, g.dma_start(out=w_cur[:, dc * D:(dc + 1) * D],
                                    in_=wk_d.ap()[dc * 128:(dc + 1) * 128, :]), 16)
            g.wait_ge(te.h, te_kg(31))
            for dc in range(DC):
                inc(dW, g.dma_start(out=w_cur[:, dc * D:(dc + 1) * D],
                                    in_=wv_d.ap()[dc * 128:(dc + 1) * 128, :]), 16)
            assert dW.n == dW_wv
            g.wait_ge(dKe[0].h, 16 * 32)
            g.wait_ge(dKe[1].h, 16 * 32)
            inc(cc, g.collective_compute(
                "AllGather", ALU.bypass, replica_groups=[list(range(NCORES))],
                ins=[ccKVi.ap().opt()], outs=[ccKVo.ap().opt()]))
            for i in range(96):
                wi, wt = divmod(i, NWT)
                buf = wstr0 if i % 2 == 0 else wstr1
                if i == 0:
                    g.wait_ge(sw.h, 96)
                if i >= 2:
                    g.wait_ge(vw.h, i - 1)
                inc(dGb[i % 2], g.dma_start(out=buf[:], in_=wtile_ap(wi, wt)), 16)
        cc_kv = cc.n

        teC0, veC0, seC0 = te.n, ve.n, se.n

        it_list = []
        for b in range(NB):
            for j in range(b, NB):
                it_list.append((b, j))
        assert len(it_list) == N_IT

        def cum(b):
            return sum(NB - i for i in range(b))

        te_S = {}
        te_TR = {}
        te_AV = {}
        tcnt = teC0
        order = []
        for k in range(min(2, N_IT)):
            order.append(("S", k))
        for k in range(N_IT):
            order.append(("TR", k))
            order.append(("AV", k))
            if k + 2 < N_IT:
                order.append(("S", k + 2))
        for kind, k in order:
            tcnt += 1
            {"S": te_S, "TR": te_TR, "AV": te_AV}[kind][k] = tcnt

        ve1 = {}
        ve2 = {}
        vcnt = veC0
        for k in range(min(2, N_IT)):
            vcnt += 1
            ve1[k] = vcnt
        for k in range(N_IT):
            vcnt += 1
            ve2[k] = vcnt
            if k + 2 < N_IT:
                vcnt += 1
                ve1[k + 2] = vcnt
        se1 = {}
        se2 = {}
        scnt = seC0
        for k in range(N_IT):
            if k >= 1:
                scnt += 1
                se2[k - 1] = scnt
            scnt += 1
            se1[k] = scnt
        scnt += 1
        se2[N_IT - 1] = scnt

        wt_pairs = []
        for wi, (src, dst, prow) in enumerate([
                (wgt_d, wgT_d, 1024), (wut_d, wuT_d, 1024), (wdt_d, wdT_d, 1024)]):
            nchk = src.shape[1] // 128
            npc = src.shape[0] // prow
            colchunks = src.shape[1] // 1024
            for ck in range(nchk):
                for pc in range(npc):
                    last_tile = wi * NWT + ((pc + 1) * prow // 128) * colchunks - 1
                    g0 = 16 * (last_tile // 2 + 1)
                    g1v = 16 * ((last_tile + 1) // 2)
                    wt_pairs.append((src, dst, prow, ck, pc, g0, g1v))
        NPAIR = len(wt_pairs)
        assert NPAIR == 32 + 32 + 32
        wt_wr_mark = {}
        wt_rd_mark = {}
        def pair_L(pr):
            g0p = pr[5] // 16
            return 2 * g0p - 1
        order_p = sorted(range(NPAIR), key=lambda p: (pair_L(wt_pairs[p]), p))
        pair_at_iter = {}
        it = 0
        for p in order_p:
            it = max(pair_L(wt_pairs[p]) + 2, it + 1)
            pair_at_iter[it] = p
        assert max(pair_at_iter) < N_IT

        def sec_of_beat(i):
            b = 0
            while b + 1 < NB and cum(b) - 1 <= i:
                b += 1
            return b
        sec_stores = {b: [] for b in range(NB)}
        for i in range(96):
            sec_stores[sec_of_beat(i)].append(i)
        sec_reads = {b: [] for b in range(NB + 2)}
        for p in sorted(range(NPAIR), key=lambda p: (pair_L(wt_pairs[p]), p)):
            sec_reads[sec_of_beat(pair_L(wt_pairs[p]))].append(p)
        pend_wr2 = []

        def emit_pair_read2(sy, kp):
            seq = len(wt_rd_mark)
            while pend_wr2 and pend_wr2[0][0] <= seq - 2:
                emit_pair_write2(sy)
            src, dst, prow, ck, pc, g0, g1v = wt_pairs[kp]
            pr2 = seq % 2
            half = wtmp[:, 0:prow] if pr2 == 0 else wtmp[:, 1024:1024 + prow]
            sy.wait_ge(dTe[0].h, g0)
            if g1v > 0:
                sy.wait_ge(dTe[1].h, g1v)
            if seq >= 2:
                sy.wait_ge(dA2[pr2].h, wt_wr_mark[seq - 2])
            inc(dA2[pr2], sy.dma_start_transpose(
                out=half,
                in_=src.ap()[pc * prow:(pc + 1) * prow,
                             ck * 128:(ck + 1) * 128]), 16)
            wt_rd_mark[seq] = dA2[pr2].n
            pend_wr2.append((seq, kp))

        def emit_pair_write2(sy):
            seq, kp = pend_wr2.pop(0)
            src, dst, prow, ck, pc, g0, g1v = wt_pairs[kp]
            pr2 = seq % 2
            half = wtmp[:, 0:prow] if pr2 == 0 else wtmp[:, 1024:1024 + prow]
            sy.wait_ge(dA2[pr2].h, wt_rd_mark[seq])
            inc(dA2[pr2], sy.dma_start(
                out=dst.ap()[ck * 128:(ck + 1) * 128,
                             pc * prow:(pc + 1) * prow],
                in_=half), 16)
            wt_wr_mark[seq] = dA2[pr2].n

        @block.sync
        def _(sy):
            sy.wait_ge(cc.h, cc_kv)
            for b in range(NB):
                if b >= 2:
                    sy.wait_ge(te.h, te_AV[cum(b - 1) - 1])
                rk, co = divmod(b, 2)
                kb = kblk2[b % 2]
                vb = vblk2[b % 2]
                for dc in range(DC):
                    inc(dKV[b % 2], sy.dma_start(
                        out=kb[:, dc * KB:(dc + 1) * KB],
                        in_=bass.AP(ccKVo,
                                    rk * 2 * KVSZ + dc * 128 * TPC + co * KB,
                                    [[TPC, 128], [1, KB]])), 16)
                for kc in range(DC):
                    inc(dKV[b % 2], sy.dma_start(
                        out=vb[:, kc * KB:(kc + 1) * KB],
                        in_=bass.AP(ccKVo,
                                    rk * 2 * KVSZ + KVSZ + (co * KB + kc * 128) * D,
                                    [[D, 128], [1, KB]])), 16)
                while pend_wr2:
                    emit_pair_write2(sy)
                for p in sec_reads.get(b, []):
                    emit_pair_read2(sy, p)
            for b in range(NB, NB + 2):
                for p in sec_reads.get(b, []):
                    emit_pair_read2(sy, p)
            while pend_wr2:
                emit_pair_write2(sy)
        assert dKV[0].n == (NB // 2) * 256 and dKV[1].n == (NB // 2) * 256
        assert len(wt_wr_mark) == NPAIR
        dA_wT = (dA2[0].n, dA2[1].n)

        def emit_S(t, k):
            b, j = it_list[k]
            t.wait_ge(dKV[b % 2].h, 256 * (b // 2 + 1))
            if k >= 2:
                t.wait_ge(se.h, se1[k - 2])
            sp = ps_b0 if k % 2 == 0 else ps_b1
            kb = kblk2[b % 2]
            for dc in range(DC):
                lhs = QT[:, dc * TPC + j * 128: dc * TPC + (j + 1) * 128]
                t.matmul(sp[:, 0:512], lhs,
                         kb[:, dc * KB: dc * KB + 512],
                         start=(dc == 0), stop=(dc == DC - 1))
                mm = t.matmul(sp[:, 512:1024], lhs,
                              kb[:, dc * KB + 512: (dc + 1) * KB],
                              start=(dc == 0), stop=(dc == DC - 1))
            inc(te, mm)
            assert te.n == te_S[k]

        def emit_TR(t, k):
            t.wait_ge(se.h, se1[k])
            off = (k % 2) * KB
            for kc in range(DC):
                tr = t.transpose(ps_c[:, off + kc * 128: off + (kc + 1) * 128],
                                 p_sb[:, off + kc * 128: off + (kc + 1) * 128],
                                 ident[:])
            inc(te, tr)
            assert te.n == te_TR[k]

        def emit_AV(t, k):
            b, j = it_list[k]
            t.wait_ge(se.h, se2[k])
            if k >= 1:
                t.wait_ge(ve.h, ve2[k - 1])
            off = (k % 2) * KB
            vb = vblk2[b % 2]
            for kc in range(DC):
                lhs = pT_sb[:, off + kc * 128: off + (kc + 1) * 128]
                t.matmul(ps_av[:, 0:512], lhs,
                         vb[:, kc * KB: kc * KB + 512],
                         start=(kc == 0), stop=(kc == DC - 1))
                mm = t.matmul(ps_av[:, 512:1024], lhs,
                              vb[:, kc * KB + 512: (kc + 1) * KB],
                              start=(kc == 0), stop=(kc == DC - 1))
            inc(te, mm)
            assert te.n == te_AV[k]

        @block.tensor
        def _(t):
            for kind, k in order:
                if kind == "S":
                    emit_S(t, k)
                elif kind == "TR":
                    emit_TR(t, k)
                else:
                    emit_AV(t, k)
        assert te.n == tcnt

        def emit_part1(v, k):
            b, j = it_list[k]
            first = (b == 0)
            sp = ps_b0 if k % 2 == 0 else ps_b1
            pr = k % 2
            v.wait_ge(te.h, te_S[k])
            if j == b:
                v.tensor_add(sp[:], sp[:], mask[:])
                v.drain()
            v.reduce_max(rowmax[:, 0:1], sp[:], axis=AX.X)
            v.drain()
            if first:
                v.tensor_copy(m_sb[:, j:j + 1], rowmax[:, 0:1])
                v.drain()
            else:
                v.tensor_scalar(d2[:, pr:pr + 1], m_sb[:, j:j + 1],
                                rowmax[:, 0:1], 0.0, ALU.subtract, ALU.min)
                v.drain()
                v.tensor_single_scalar(m_sb[:, j:j + 1], m_sb[:, j:j + 1],
                                       rowmax[:, 0:1], ALU.max)
                v.drain()
            v.tensor_scalar_mul(nbias[:, pr:pr + 1], m_sb[:, j:j + 1], -SCL)
            inc(ve, v.drain())
            assert ve.n == ve1[k]

        def emit_part2(v, k):
            b, j = it_list[k]
            first = (b == 0)
            pr = k % 2
            v.wait_ge(te.h, te_AV[k])
            if first:
                v.tensor_copy(N_sb[:, j * D:(j + 1) * D], ps_av[:])
                v.drain()
                v.tensor_copy(z_sb[:, j:j + 1], zblk[:, pr:pr + 1])
            else:
                v.tensor_scalar_min(cfac[:, pr:pr + 1], cfac_a[:, pr:pr + 1], 1.0)
                v.drain()
                v.scalar_tensor_tensor(N_sb[:, j * D:(j + 1) * D],
                                       N_sb[:, j * D:(j + 1) * D],
                                       cfac[:, pr:pr + 1],
                                       ps_av[:], ALU.mult, ALU.add)
                v.drain()
                v.scalar_tensor_tensor(z_sb[:, j:j + 1], z_sb[:, j:j + 1],
                                       cfac[:, pr:pr + 1],
                                       zblk[:, pr:pr + 1], ALU.mult, ALU.add)
            inc(ve, v.drain())
            assert ve.n == ve2[k]

        def emit_tern(v, k):
            if k < 96:
                i = k
                wi, wt = divmod(i, NWT)
                buf = wstr0 if i % 2 == 0 else wstr1
                trn = wtrn[:, 0:1024] if i % 2 == 0 else wtrn[:, 1024:2048]
                if i == 0:
                    v.wait_ge(dS.h, dS_means)
                v.wait_ge(dGb[i % 2].h, 16 * (i // 2 + 1))
                if i >= 2:
                    v.wait_ge(dTe[i % 2].h, dTe_tern(i - 2))
                v.tensor_single_scalar(wneg[:], buf[:],
                                       wthr6[:, 2 * wi + 1:2 * wi + 2], ALU.is_lt)
                v.drain()
                v.scalar_tensor_tensor(trn, buf[:],
                                       wthr6[:, 2 * wi:2 * wi + 1], wneg[:],
                                       ALU.is_gt, ALU.subtract)
                inc(vw, v.drain())

        @block.vector
        def _(v):
            for k in range(min(2, N_IT)):
                emit_part1(v, k)
            for k in range(N_IT):
                emit_part2(v, k)
                if k + 2 < N_IT:
                    emit_part1(v, k + 2)
                emit_tern(v, k)
        assert ve.n == vcnt
        assert vw.n == 96

        def emit_expgrp(s, k):
            b, j = it_list[k]
            first = (b == 0)
            sp = ps_b0 if k % 2 == 0 else ps_b1
            pr = k % 2
            s.wait_ge(ve.h, ve1[k])
            if not first:
                s.activation(cfac_a[:, pr:pr + 1], d2[:, pr:pr + 1],
                             AF.Exp, scale=SCL)
                s.drain()
            s.activation(p_sb[:, pr * KB:(pr + 1) * KB], sp[:], AF.Exp,
                         bias=nbias[:, pr:pr + 1], scale=SCL,
                         accum_out=zblk[:, pr:pr + 1])
            inc(se, s.drain())
            assert se.n == se1[k]

        def emit_ptcopy(s, k):
            pr = k % 2
            s.wait_ge(te.h, te_TR[k])
            s.activation(pT_sb[:, pr * KB:(pr + 1) * KB],
                         ps_c[:, pr * KB:(pr + 1) * KB], AF.Copy)
            inc(se, s.drain())
            assert se.n == se2[k]

        pend_wr = []

        def emit_pair_read(s, kp):
            src, dst, prow, ck, pc, g0, g1v = wt_pairs[kp]
            seq = len(wt_rd_mark)
            pr2 = seq % 2
            half = wtmp[:, 0:prow] if pr2 == 0 else wtmp[:, 1024:1024 + prow]
            s.wait_ge(dTe[0].h, g0)
            if g1v > 0:
                s.wait_ge(dTe[1].h, g1v)
            if seq >= 2:
                s.wait_ge(dA2[pr2].h, wt_wr_mark[seq - 2])
            inc(dA2[pr2], s.dma_start_transpose(
                out=half,
                in_=src.ap()[pc * prow:(pc + 1) * prow,
                             ck * 128:(ck + 1) * 128]), 16)
            wt_rd_mark[seq] = dA2[pr2].n
            pend_wr.append((seq, kp))

        def emit_pair_write(s):
            seq, kp = pend_wr.pop(0)
            src, dst, prow, ck, pc, g0, g1v = wt_pairs[kp]
            pr2 = seq % 2
            half = wtmp[:, 0:prow] if pr2 == 0 else wtmp[:, 1024:1024 + prow]
            s.wait_ge(dA2[pr2].h, wt_rd_mark[seq])
            inc(dA2[pr2], s.dma_start(
                out=dst.ap()[ck * 128:(ck + 1) * 128,
                             pc * prow:(pc + 1) * prow],
                in_=half), 16)
            wt_wr_mark[seq] = dA2[pr2].n

        @block.scalar
        def _(s):
            for k in range(N_IT):
                if 3 <= k < 99:
                    i = k - 3
                    trn = wtrn[:, 0:1024] if i % 2 == 0 else wtrn[:, 1024:2048]
                    wi, wt = divmod(i, NWT)
                    s.wait_ge(vw.h, i + 1)
                    inc(dTe[i % 2], s.dma_start(out=wtern_ap(wi, wt),
                                                in_=trn), 16)
                if k >= 1:
                    emit_ptcopy(s, k - 1)
                emit_expgrp(s, k)
            emit_ptcopy(s, N_IT - 1)
        assert se.n == scnt
        assert dTe[0].n == 16 * 48 and dTe[1].n == 16 * 48
        teC_end, veC_end, seC_end = te.n, ve.n, se.n

        @block.vector
        def _(v):
            v.wait_ge(ve.h, veC_end)
            v.reduce_max(t1[:, 0:1], m_sb[:], axis=AX.X)
            inc(ve, v.drain())
        vS1 = ve.n

        @block.sync
        def _(sy):
            sy.wait_ge(ve.h, vS1)
            inc(dS, sy.dma_start(out=col_d.ap()[:], in_=t1[:, 0:1]), 16)
            sy.wait_ge(dS.h, dS.n)
            inc(dS, sy.dma_start(out=strow[:, :], in_=col_d.ap()[:]), 16)
        dS_S1 = dS.n

        @block.vector
        def _(v):
            v.wait_ge(dS.h, dS_S1)
            v.reduce_max(g1[:, 0:1], strow[:, :], axis=AX.X)
            inc(ve, v.drain())
        vS2 = ve.n

        @block.sync
        def _(sy):
            sy.wait_ge(ve.h, vS2)
            inc(dS, sy.dma_start(out=sc2_d.ap()[0:1], in_=g1[:, 0:1]), 16)
            sy.wait_ge(dS.h, dS.n)
            inc(dS, sy.dma_start(
                out=MgB[:], in_=bass.AP(sc2_d, 0, [[0, 128], [1, 1]])), 16)
        dS_S2 = dS.n

        @block.vector
        def _(v):
            v.wait_ge(dS.h, dS_S2)
            v.tensor_scalar_mul(nbg[:, 0:1], MgB[:, 0:1], -SCL)
            inc(ve, v.drain())
        vS3 = ve.n

        sS1 = seC_end + 1

        @block.vector
        def _(v):
            v.wait_ge(se.h, sS1)
            v.tensor_mul(ef16[:], ef16[:], z_sb[:])
            v.drain()
            v.reduce_sum(t1[:, 0:1], ef16[:], axis=AX.X)
            inc(ve, v.drain())
        vS4 = ve.n

        @block.sync
        def _(sy):
            sy.wait_ge(ve.h, vS4)
            inc(dS, sy.dma_start(out=col_d.ap()[:], in_=t1[:, 0:1]), 16)
            sy.wait_ge(dS.h, dS.n)
            inc(dS, sy.dma_start(out=strow[:, :], in_=col_d.ap()[:]), 16)
        dS_S3 = dS.n

        @block.vector
        def _(v):
            v.wait_ge(dS.h, dS_S3)
            v.reduce_sum(g2[:, 0:1], strow[:, :], axis=AX.X)
            v.drain()
            v.tensor_copy(stg2[:, 0:1], g1[:, 0:1])
            v.drain()
            v.tensor_copy(stg2[:, 1:2], g2[:, 0:1])
            inc(ve, v.drain())
        vS5 = ve.n

        @block.sync
        def _(sy):
            sy.wait_ge(ve.h, vS5)
            inc(dS, sy.dma_start(out=sti.ap()[:, :], in_=stg2[:, :]), 16)
        dS_S4 = dS.n

        @block.gpsimd
        def _(g):
            g.wait_ge(dS.h, dS_S4)
            inc(cc, g.collective_compute(
                "AllGather", ALU.bypass, replica_groups=[list(range(NCORES))],
                ins=[sti.ap().opt()], outs=[sto_d.ap().opt()]))
            g.wait_ge(cc.h, cc.n)
            inc(dG, g.dma_start(
                out=strow2[:, :],
                in_=bass.AP(sto_d, 0, [[128, 1], [1, 128]])), 16)
        dG_S = dG.n

        @block.vector
        def _(v):
            v.wait_ge(dG.h, dG_S)
            v.tensor_copy(m8[:, :], bass.AP(strow2, 0, [[128, 1], [16, 8]]))
            v.drain()
            v.tensor_copy(z8[:, :], bass.AP(strow2, 1, [[128, 1], [16, 8]]))
            v.drain()
            v.reduce_max(g1[:, 0:1], m8[:, :], axis=AX.X)
            v.drain()
            v.tensor_scalar_mul(g2[:, 0:1], g1[:, 0:1], -SCL)
            inc(ve, v.drain())
        vS6 = ve.n

        sS2 = seC_end + 2

        @block.vector
        def _(v):
            v.wait_ge(se.h, sS2)
            v.tensor_mul(e8[:, :], e8[:, :], z8[:, :])
            v.drain()
            v.reduce_sum(g2[:, 0:1], e8[:, :], axis=AX.X)
            v.drain()
            v.tensor_copy(stg2[:, 2:3], g1[:, 0:1])
            v.drain()
            v.tensor_copy(stg2[:, 3:4], g2[:, 0:1])
            inc(ve, v.drain())
        vS7 = ve.n

        @block.sync
        def _(sy):
            sy.wait_ge(ve.h, vS7)
            inc(dS, sy.dma_start(out=sc2_d.ap()[2:4], in_=stg2[:, 2:4]), 16)
            sy.wait_ge(dS.h, dS.n)
            inc(dS, sy.dma_start(
                out=MgB[:], in_=bass.AP(sc2_d, 2, [[0, 128], [1, 1]])), 16)
            inc(dS, sy.dma_start(
                out=ZgB[:], in_=bass.AP(sc2_d, 3, [[0, 128], [1, 1]])), 16)
        dS_S5 = dS.n

        @block.vector
        def _(v):
            v.wait_ge(dS.h, dS_S5)
            v.reciprocal(rz[:, 0:1], ZgB[:, 0:1])
            v.drain()
            v.tensor_scalar_mul(nbg[:, 0:1], MgB[:, 0:1], -SCL)
            inc(ve, v.drain())
        vS8 = ve.n

        sS3 = seC_end + 3

        @block.vector
        def _(v):
            v.wait_ge(se.h, sS3)
            v.tensor_scalar_mul(f_sb[:], ef16[:], rz[:, 0:1])
            inc(ve, v.drain())
        vS_f = ve.n

        d_v1 = {}
        d_v2 = {}
        d_done = {}
        d_tint = {}
        seD0 = seC_end + 3
        for j in range(NT):
            xb = xb0 if j % 2 == 0 else xb1

            @block.sync
            def _(sy, j=j, xb=xb):
                if j >= 2:
                    sy.wait_ge(ve.h, d_v2[j - 2])
                inc(dX[j % 2], sy.dma_start(
                    out=xb[:], in_=x_own_d.ap()[j * 128:(j + 1) * 128, :]), 16)
            _dj_x = (j % 2, dX[j % 2].n)

            @block.vector
            def _(v, j=j, djx=_dj_x):
                v.wait_ge(ve.h, vS_f)
                v.wait_ge(dX[djx[0]].h, djx[1])
                v.tensor_scalar_mul(scrA[:], N_sb[:, j * D:(j + 1) * D],
                                    f_sb[:, j:j + 1])
                v.drain()
                v.bn_stats(st12[:, 0:6], scrA[:, 0:512])
                v.bn_stats(st12[:, 6:12], scrA[:, 512:1024])
                v.drain()
                v.bn_aggr(mv[:], st12[:])
                inc(ve, v.drain())
            _vj1 = ve.n
            d_v1[j] = _vj1
            _sj1 = seD0 + 2 * j + 1

            @block.vector
            def _(v, j=j, xb=xb, s1=_sj1):
                v.wait_ge(se.h, s1)
                v.reciprocal(istd[:], veps[:])
                v.drain()
                v.tensor_scalar(scrB[:], scrA[:], mv[:, 0:1], istd[:, 0:1],
                                ALU.subtract, ALU.mult)
                v.drain()
                v.tensor_add(scrA[:], xb[:], scrB[:])
                v.drain()
                v.bn_stats(st12[:, 0:6], scrA[:, 0:512])
                v.bn_stats(st12[:, 6:12], scrA[:, 512:1024])
                v.drain()
                v.bn_aggr(mv[:], st12[:])
                inc(ve, v.drain())
            _vj2 = ve.n
            d_v2[j] = _vj2

            @block.sync
            def _(sy, j=j, v2=_vj2):
                sy.wait_ge(ve.h, v2)
                inc(dS, sy.dma_start(
                    out=xres_d.ap()[j * 128:(j + 1) * 128, :], in_=scrA[:]), 16)
            _dj_xr = dS.n
            _sj2 = seD0 + 2 * j + 2

            tbj = tb_bf if j % 2 == 0 else tb2_bf

            @block.vector
            def _(v, j=j, s2=_sj2, dxr=_dj_xr, tbj=tbj):
                v.wait_ge(se.h, s2)
                v.wait_ge(dS.h, dxr)
                if j >= 2:
                    v.wait_ge(dTi[j % 2].h, d_tint[j - 2])
                v.reciprocal(istd[:], veps[:])
                v.drain()
                v.tensor_scalar(scrB[:], scrA[:], mv[:, 0:1], istd[:, 0:1],
                                ALU.subtract, ALU.mult)
                v.drain()
                v.reduce_max(amax[:, 0:1], scrB[:], axis=AX.X,
                             apply_absolute_value=True)
                v.drain()
                v.tensor_scalar_max(acl[:, 0:1], amax[:, 0:1], 1e-5)
                v.drain()
                v.reciprocal(t1[:, 0:1], acl[:, 0:1])
                v.drain()
                v.tensor_scalar_mul(a_sb[:, j:j + 1], t1[:, 0:1], 127.0)
                v.drain()
                v.tensor_scalar(scg_sb[:, j:j + 1], acl[:, 0:1], mwg_b[:, 0:1],
                                1.0 / 127.0, ALU.mult, ALU.mult)
                v.drain()
                v.tensor_scalar(scu_sb[:, j:j + 1], acl[:, 0:1], mwu_b[:, 0:1],
                                1.0 / 127.0, ALU.mult, ALU.mult)
                v.drain()
                v.tensor_scalar_mul(xcs[:], scrB[:], a_sb[:, j:j + 1])
                v.drain()
                v.tensor_copy(iscr[:], xcs[:])
                v.drain()
                v.tensor_copy(tbj[:], iscr[:])
                inc(ve, v.drain())
            _vj3 = ve.n

            @block.sync
            def _(sy, j=j, v3=_vj3, tbj=tbj):
                sy.wait_ge(ve.h, v3)
                inc(dTi[j % 2], sy.dma_start(
                    out=tint_d.ap()[j * 128:(j + 1) * 128, :], in_=tbj[:]), 16)
            d_tint[j] = dTi[j % 2].n
            d_done[j] = _vj3
        ve_D = d_done[NT - 1]

        @block.scalar
        def _(s):
            s.wait_ge(ve.h, vS3)
            s.activation(ef16[:], m_sb[:], AF.Exp, bias=nbg[:, 0:1], scale=SCL)
            inc(se, s.drain())
            s.wait_ge(ve.h, vS6)
            s.activation(e8[:, :], m8[:, :], AF.Exp, bias=g2[:, 0:1], scale=SCL)
            inc(se, s.drain())
            s.wait_ge(ve.h, vS8)
            s.activation(ef16[:], m_sb[:], AF.Exp, bias=nbg[:, 0:1], scale=SCL)
            inc(se, s.drain())
            for j in range(NT):
                s.wait_ge(ve.h, d_v1[j])
                s.activation(veps[:], mv[:, 1:2], AF.Sqrt, bias=eps_sb[:, 0:1])
                inc(se, s.drain())
                s.wait_ge(ve.h, d_v2[j])
                s.activation(veps[:], mv[:, 1:2], AF.Sqrt, bias=eps_sb[:, 0:1])
                inc(se, s.drain())
        assert se.n == seC_end + 3 + 2 * NT

        @block.scalar
        def _(s):
            s.wait_ge(ve.h, ve_D)
            s.wait_ge(dA2[0].h, dA_wT[0])
            s.wait_ge(dA2[1].h, dA_wT[1])
            for ic in range(ICN):
                inc(dWd, s.dma_start(
                    out=wdT_sb[:, ic * D:(ic + 1) * D],
                    in_=wdT_d.ap()[ic * 128:(ic + 1) * 128, :]), 16)
        dWd_mark = dWd.n

        e1_te_slice = {}
        e1_ve_h = {}
        e1_te_g = {}
        e2_v1 = {}
        e2_vo = {}
        so_marks = {}
        seE0 = seC_end + 3 + 2 * NT
        bounce_prev = [d_tint[NT - 2], d_tint[NT - 1]]
        hT_prev = [dKV[0].n, dKV[1].n]
        bseq = [0]
        tT_mark = {}

        def emit_tintT(ch):
            rows0 = ch * CHT

            @block.sync
            def _(sy, ch=ch, rows0=rows0):
                sy.wait_ge(dTi[0].h, d_tint[4 * ch + 2])
                sy.wait_ge(dTi[1].h, d_tint[4 * ch + 3])
                if ch >= 1:
                    sy.wait_ge(te.h, e1_te_slice[ch * NIN - 1])
                for dc in range(DC):
                    inc(dTT, sy.dma_start_transpose(
                        out=tintT[:, dc * CHT:(dc + 1) * CHT],
                        in_=tint_d.ap()[rows0:rows0 + CHT,
                                        dc * 128:(dc + 1) * 128]), 16)
            tT_mark[ch] = dTT.n

        emit_tintT(0)
        for ch in range(NCH):
            rows0 = ch * CHT
            hq_mark = {}
            hq_last = {}
            se0 = seE0 + 36 * ch
            _dS_tT = tT_mark[ch]

            for n in range(NIN):
                sidx = ch * NIN + n
                par = n % 2

                @block.sync
                def _(sy, n=n, par=par, sidx=sidx):
                    if sidx >= 2:
                        sy.wait_ge(te.h, e1_te_slice[sidx - 2])
                    else:
                        sy.wait_ge(te.h, teC_end)
                        sy.wait_ge(dA2[0].h, dA_wT[0])
                        sy.wait_ge(dA2[1].h, dA_wT[1])
                    for dc in range(DC):
                        inc(dWn[par], sy.dma_start(
                            out=wgn[par][:, dc * 512:(dc + 1) * 512],
                            in_=wgT_d.ap()[dc * 128:(dc + 1) * 128,
                                           n * 512:(n + 1) * 512]), 16)

                @block.gpsimd
                def _(g, n=n, par=par, sidx=sidx):
                    if sidx >= 2:
                        g.wait_ge(te.h, e1_te_slice[sidx - 2])
                    else:
                        g.wait_ge(te.h, teC_end)
                        g.wait_ge(dA2[0].h, dA_wT[0])
                        g.wait_ge(dA2[1].h, dA_wT[1])
                    for dc in range(DC):
                        inc(dW8[par], g.dma_start(
                            out=wun[par][:, dc * 512:(dc + 1) * 512],
                            in_=wuT_d.ap()[dc * 128:(dc + 1) * 128,
                                           n * 512:(n + 1) * 512]), 16)
                _dS_wn = (par, dWn[par].n, dW8[par].n)

                for tl in range(CTL):
                    gidx = ch * CTL + tl
                    gk = sidx * CTL + tl
                    gps = ps_b0 if gk % 2 == 0 else ps_b1

                    @block.tensor
                    def _(t, tl=tl, par=par, gps=gps, gk=gk, ch=ch,
                          dwn=_dS_wn, dtT=_dS_tT):
                        t.wait_ge(dTT.h, dtT)
                        t.wait_ge(dWn[dwn[0]].h, dwn[1])
                        t.wait_ge(dW8[dwn[0]].h, dwn[2])
                        if gk >= 2:
                            t.wait_ge(ve.h, e1_ve_h[gk - 2])
                        if ch > 0 and gk < ch * NIN * CTL + 2:
                            t.wait_ge(ve.h, e2_v1[4 * (ch - 1) + 2 + (gk % 2)])
                        for dc in range(DC):
                            lhs = tintT[:, dc * CHT + tl * 128: dc * CHT + (tl + 1) * 128]
                            t.matmul(gps[:, 0:512], lhs,
                                     wgn[par][:, dc * 512:(dc + 1) * 512],
                                     start=(dc == 0), stop=(dc == DC - 1))
                            mm = t.matmul(gps[:, 512:1024], lhs,
                                          wun[par][:, dc * 512:(dc + 1) * 512],
                                          start=(dc == 0), stop=(dc == DC - 1))
                        inc(te, mm)
                    _te_g = te.n
                    e1_te_slice[sidx] = te.n
                    gkl = n * CTL + tl
                    e1_te_g[(ch, gkl)] = te.n

                    gb = g_bf[:, (gk % 2) * 512:(gk % 2) * 512 + 512]
                    _se_g = se0 + gkl + 1

                    @block.vector
                    def _(v, n=n, tl=tl, gidx=gidx, gps=gps, tg=_te_g,
                          sg=_se_g, gb=gb):
                        v.wait_ge(te.h, tg)
                        v.wait_ge(se.h, sg)
                        v.scalar_tensor_tensor(
                            h_sb[:, tl * INNER + n * 512: tl * INNER + (n + 1) * 512],
                            gps[:, 512:1024], scu_sb[:, gidx:gidx + 1], gb,
                            ALU.mult, ALU.mult)
                        inc(ve, v.drain())
                    e1_ve_h[gk] = ve.n

                    if n == NIN - 1:
                        @block.vector
                        def _(v, tl=tl, gidx=gidx):
                            v.reduce_max(amax[:, 0:1],
                                         h_sb[:, tl * INNER:(tl + 1) * INNER],
                                         axis=AX.X, apply_absolute_value=True)
                            v.drain()
                            v.tensor_scalar_max(acl[:, 0:1],
                                                amax[:, 0:1], 1e-5)
                            v.drain()
                            v.reciprocal(t1[:, 0:1], acl[:, 0:1])
                            v.drain()
                            v.tensor_scalar_mul(ah_s[:, 0:1], t1[:, 0:1], 127.0)
                            v.drain()
                            v.tensor_scalar(cd_sb[:, gidx:gidx + 1], acl[:, 0:1],
                                            mwd_b[:, 0:1], 1.0 / 127.0,
                                            ALU.mult, ALU.mult)
                            inc(ve, v.drain())

                        for sl in range(4):
                            bpar = bseq[0] % 2
                            bseq[0] += 1
                            tbq = tb_bf if bpar == 0 else tb2_bf
                            prev = bounce_prev[bpar]

                            @block.vector
                            def _(v, tl=tl, sl=sl, tbq=tbq, prev=prev,
                                  bpar=bpar):
                                v.wait_ge(dTi[bpar].h, prev)
                                v.tensor_scalar_mul(
                                    xcs[:],
                                    h_sb[:, tl * INNER + sl * 1024: tl * INNER + (sl + 1) * 1024],
                                    ah_s[:, 0:1])
                                v.drain()
                                v.tensor_copy(iscr[:], xcs[:])
                                v.drain()
                                v.tensor_copy(tbq[:], iscr[:])
                                inc(ve, v.drain())
                            _qv = ve.n

                            @block.sync
                            def _(sy, ch=ch, tl=tl, sl=sl, tbq=tbq, qv=_qv,
                                  bpar=bpar):
                                sy.wait_ge(ve.h, qv)
                                inc(dTi[bpar], sy.dma_start(
                                    out=hint_d.ap()[ch * CHT + tl * 128: ch * CHT + (tl + 1) * 128,
                                                    sl * 1024:(sl + 1) * 1024],
                                    in_=tbq[:]), 16)
                            bounce_prev[bpar] = dTi[bpar].n
                            hq_mark[(tl, sl)] = (bpar, dTi[bpar].n)
                            hq_last[(sl, bpar)] = dTi[bpar].n

            if ch + 1 < NCH:
                emit_tintT(ch + 1)

            hTm = []
            for ic in range(ICN):
                hp = ic % 2

                @block.sync
                def _(sy, ic=ic, rows0=rows0, hq=dict(hq_last), hp=hp,
                      hprev=hT_prev[hp]):
                    for qp in (0, 1):
                        if (ic // 8, qp) in hq:
                            sy.wait_ge(dTi[qp].h, hq[(ic // 8, qp)])
                    if hprev:
                        sy.wait_ge(dKV[hp].h, hprev)
                    inc(dKV[hp], sy.dma_start_transpose(
                        out=hT_sb[:, ic * CHT:(ic + 1) * CHT],
                        in_=hint_d.ap()[rows0:rows0 + CHT,
                                        ic * 128:(ic + 1) * 128]), 16)
                hT_prev[hp] = dKV[hp].n
                hTm.append((hp, dKV[hp].n))

            for tl in range(CTL):
                gidx = ch * CTL + tl
                psd = ps_b0 if tl % 2 == 0 else ps_b1

                @block.sync
                def _(sy, gidx=gidx):
                    if gidx >= 2:
                        sy.wait_ge(ve.h, e2_vo[gidx - 2])
                    else:
                        sy.wait_ge(ve.h, d_v2[NT - 2 + gidx])
                    xb = xb0 if gidx % 2 == 0 else xb1
                    inc(dX[gidx % 2], sy.dma_start(
                        out=xb[:],
                        in_=xres_d.ap()[gidx * 128:(gidx + 1) * 128, :]), 16)
                _dS_xr = (gidx % 2, dX[gidx % 2].n)

                @block.tensor
                def _(t, tl=tl, gidx=gidx, psd=psd, hTm=tuple(hTm),
                      dwd=dWd_mark):
                    t.wait_ge(dWd.h, dwd)
                    if tl >= 2:
                        t.wait_ge(ve.h, e2_v1[gidx - 2])
                    if tl > 0:
                        t.wait_ge(dKV[hTm[-2][0]].h, hTm[-2][1])
                        t.wait_ge(dKV[hTm[-1][0]].h, hTm[-1][1])
                    for ic in range(ICN):
                        if tl == 0:
                            t.wait_ge(dKV[hTm[ic][0]].h, hTm[ic][1])
                        lhs = hT_sb[:, ic * CHT + tl * 128: ic * CHT + (tl + 1) * 128]
                        t.matmul(psd[:, 0:512], lhs,
                                 wdT_sb[:, ic * D: ic * D + 512],
                                 start=(ic == 0), stop=(ic == ICN - 1))
                        mm = t.matmul(psd[:, 512:1024], lhs,
                                      wdT_sb[:, ic * D + 512:(ic + 1) * D],
                                      start=(ic == 0), stop=(ic == ICN - 1))
                    inc(te, mm)
                _e2_te = te.n

                @block.vector
                def _(v, gidx=gidx, psd=psd, te_m=_e2_te):
                    v.wait_ge(te.h, te_m)
                    v.tensor_scalar_mul(scrA[:], psd[:], cd_sb[:, gidx:gidx + 1])
                    v.drain()
                    v.bn_stats(st12[:, 0:6], scrA[:, 0:512])
                    v.bn_stats(st12[:, 6:12], scrA[:, 512:1024])
                    v.drain()
                    v.bn_aggr(mv[:], st12[:])
                    inc(ve, v.drain())
                e2_v1[gidx] = ve.n

                _e2_s1 = se0 + 32 + tl + 1

                @block.vector
                def _(v, gidx=gidx, s1=_e2_s1, dxr=_dS_xr):
                    v.wait_ge(se.h, s1)
                    v.wait_ge(dX[dxr[0]].h, dxr[1])
                    if gidx >= 1:
                        v.wait_ge(so.h, so_marks[gidx - 1])
                    xb = xb0 if gidx % 2 == 0 else xb1
                    v.reciprocal(istd[:], veps[:])
                    v.drain()
                    v.tensor_scalar(scrB[:], scrA[:], mv[:, 0:1], istd[:, 0:1],
                                    ALU.subtract, ALU.mult)
                    v.drain()
                    v.tensor_add(scrB[:], scrB[:], xb[:])
                    inc(ve, v.drain())
                e2_vo[gidx] = ve.n

                @block.sync
                def _(sy, gidx=gidx, vo=e2_vo[gidx]):
                    sy.wait_ge(ve.h, vo)
                    inc(so, sy.dma_start(
                        out=out_d.ap()[gidx * 128:(gidx + 1) * 128, :],
                        in_=scrB[:]), 16)
                so_marks[gidx] = so.n

            @block.scalar
            def _(s, ch=ch):
                for gkl in range(NIN * CTL):
                    gk = ch * NIN * CTL + gkl
                    tl = gkl % CTL
                    gidx = ch * CTL + tl
                    gb = g_bf[:, (gk % 2) * 512:(gk % 2) * 512 + 512]
                    gps = ps_b0 if gk % 2 == 0 else ps_b1
                    s.wait_ge(te.h, e1_te_g[(ch, gkl)])
                    if gk >= 2:
                        s.wait_ge(ve.h, e1_ve_h[gk - 2])
                    s.activation(gb, gps[:, 0:512], AF.Silu,
                                 scale=scg_sb[:, gidx:gidx + 1])
                    inc(se, s.drain())
                for tl in range(CTL):
                    gidx = ch * CTL + tl
                    s.wait_ge(ve.h, e2_v1[gidx])
                    s.activation(veps[:], mv[:, 1:2], AF.Sqrt,
                                 bias=eps_sb[:, 0:1])
                    inc(se, s.drain())
            assert se.n == seE0 + 36 * (ch + 1), (se.n, seE0, ch)

        @block.sync
        def _(sy):
            sy.wait_ge(so.h, so.n)

    return nc


_nc_cache = None


def _get_nc():
    global _nc_cache
    if _nc_cache is None:
        _nc_cache = build()
    return _nc_cache


def kernel(**inputs):
    x = np.asarray(inputs["L0_attn_x"], dtype=np.float32).reshape(T, D)
    wq = np.ascontiguousarray(np.asarray(inputs["L0_attn_Wq"], np.float32))
    wk = np.ascontiguousarray(np.asarray(inputs["L0_attn_Wk"], np.float32))
    wv = np.ascontiguousarray(np.asarray(inputs["L0_attn_Wv"], np.float32))
    bq = np.ascontiguousarray(np.asarray(inputs["L0_attn_bq"], np.float32))
    bk = np.ascontiguousarray(np.asarray(inputs["L0_attn_bk"], np.float32))
    bv = np.ascontiguousarray(np.asarray(inputs["L0_attn_bv"], np.float32))
    wg = np.ascontiguousarray(np.asarray(inputs["L0_ffn_w_gate"], np.float32))
    wu = np.ascontiguousarray(np.asarray(inputs["L0_ffn_w_up"], np.float32))
    wd = np.ascontiguousarray(np.asarray(inputs["L0_ffn_w_down"], np.float32))

    xt = x.reshape(T // 128, 128, D)
    in_maps = []
    for c in range(NCORES):
        idx = np.arange(NT) * NCORES + c
        x_own = np.ascontiguousarray(xt[idx].reshape(TPC, D))
        x_ctg = np.ascontiguousarray(x[c * TPC:(c + 1) * TPC])
        in_maps.append({
            "x_own": x_own, "x_ctg": x_ctg,
            "wq": wq, "wk": wk, "wv": wv,
            "bq": bq, "bk": bk, "bv": bv,
            "wg": wg, "wu": wu, "wd": wd,
            "cid": np.array([[c]], np.float32),
        })

    nc = _get_nc()
    res = run_bass_kernel_spmd(nc, in_maps, core_ids=list(range(NCORES)))

    out = np.empty((T // 128, 128, D), np.float32)
    for c in range(NCORES):
        idx = np.arange(NT) * NCORES + c
        out[idx] = np.asarray(res.results[c]["out"], np.float32).reshape(NT, 128, D)
    return out.reshape(1, T, D)
